# version 01 (final)
# baseline (speedup 1.0000x reference)
import numpy as np
import ml_dtypes
from contextlib import ExitStack

import concourse.bass as bass
import concourse.mybir as mybir
from concourse.bass_utils import run_bass_kernel_spmd
from concourse.bass_core_rust import IndirectOffsetOnAxis

F32 = mybir.dt.float32
BF16 = mybir.dt.bfloat16
U32 = mybir.dt.uint32
Act = mybir.ActivationFunctionType
Alu = mybir.AluOpType
AX = mybir.AxisListType.X
BF = ml_dtypes.bfloat16

NCORE = 8
NSEQ = 2
TSEQ = 2048
T = NSEQ * TSEQ
D = 1024
NT = T // 128
NTS = TSEQ // 128
FC = D // 128
NPAIR = 8
EPS_LN = 1e-5
EPS_ATTN = 8e-6
E = 8
C_GLOBAL = 8602
C_LOC = 2048
HID = 4096


class Em:
    def __init__(self):
        self.v = {}

    def fin(self, instr, sem, amt=1):
        instr.then_inc(sem, amt)
        self.v[sem.num] = self.v.get(sem.num, 0) + amt
        return self.v[sem.num]

    def cur(self, sem):
        return self.v.get(sem.num, 0)


def host_constants():
    c = {}
    c["c_ones128"] = np.ones((128, 1), np.float32)
    c["c_onesb"] = np.ones((128, 1), np.float32).astype(BF)
    c["c_onesrow"] = np.ones((1, 128), np.float32).astype(BF)
    e2 = np.zeros((2, 128), np.float32)
    e2[0, 0:64] = 1.0
    e2[1, 64:128] = 1.0
    c["c_e2pair"] = e2.astype(BF)
    return c


def host_constants_b():
    c = {}
    c["c_ones128"] = np.ones((128, 1), np.float32)
    c["c_onesb"] = np.ones((128, 1), np.float32).astype(BF)
    c["c_onesrow"] = np.ones((1, 128), np.float32).astype(BF)
    c["c_onesrowf"] = np.ones((1, 128), np.float32)
    c["c_trils"] = np.triu(np.ones((128, 128), np.float32), 1)
    c["c_identf"] = np.eye(128, dtype=np.float32)
    c["c_trash"] = np.full((128, 1), float(E * C_LOC), np.float32) +         np.arange(128, dtype=np.float32)[:, None]
    c["c_eclrow"] = (np.arange(8, dtype=np.float32) * C_LOC)[None, :].copy()
    c["c_eclbc"] = np.broadcast_to(np.arange(8, dtype=np.float32) * C_LOC, (128, 8)).copy()
    c["c_onesrow5"] = np.ones((1, 512), np.float32).astype(BF)
    m = np.where(np.arange(128)[:, None] <= np.arange(128)[None, :], 0.0, -1e30)
    c["c_negmask4"] = np.tile(m.astype(np.float32), (1, 4)).copy()
    c["c_identb"] = np.eye(128, dtype=np.float32).astype(BF)
    selg = np.zeros((16, 2048), np.float32)
    for g in range(4):
        for j in range(4):
            selg[4 * g + j, g * 512 + j * 128:g * 512 + (j + 1) * 128] = 1.0
    c["c_selg"] = selg
    parw = np.zeros((16, 128), np.float32)
    for h in range(16):
        parw[h, (h % 2) * 64:(h % 2) * 64 + 64] = 1.0
    c["c_parw"] = parw
    selpair = np.zeros((16, 512), np.float32)
    for h in range(16):
        selpair[h, (h // 2) * 64:(h // 2) * 64 + 64] = 1.0
    c["c_selpair"] = selpair
    return c


def build_launch_a():
    nc = bass.Bass("TRN2", target_bir_lowering=False, dynamic_dma_scratch_size=16384)

    x_in = nc.dram_tensor("x_in", [T, D], F32, kind="ExternalInput").ap()
    wqkv = nc.dram_tensor("wqkv", [3 * D, D], F32, kind="ExternalInput").ap()
    wout = nc.dram_tensor("wout", [D, D], F32, kind="ExternalInput").ap()
    bout = nc.dram_tensor("bout", [1, D], F32, kind="ExternalInput").ap()
    wg = nc.dram_tensor("wg", [E, D], F32, kind="ExternalInput").ap()
    c_ones128 = nc.dram_tensor("c_ones128", [128, 1], F32, kind="ExternalInput").ap()
    c_onesb = nc.dram_tensor("c_onesb", [128, 1], BF16, kind="ExternalInput").ap()
    c_onesrow = nc.dram_tensor("c_onesrow", [1, 128], BF16, kind="ExternalInput").ap()
    c_iota8 = nc.dram_tensor("c_iota8", [128, 8], F32, kind="ExternalInput").ap()
    c_e2pair = nc.dram_tensor("c_e2pair", [2, 128], BF16, kind="ExternalInput").ap()

    x1_out = nc.dram_tensor("x1_out", [T, D], F32, kind="ExternalOutput").ap()
    xln1_out = nc.dram_tensor("xln1_out", [T, D], BF16, kind="ExternalOutput").ap()
    hist_out = nc.dram_tensor("hist_out", [2, E], F32, kind="ExternalOutput").ap()

    dbg_xln = nc.dram_tensor("dbg_xln", [T, D], BF16, kind="ExternalOutput").ap()
    dbg_kvblk = nc.dram_tensor("dbg_kvblk", [128, NPAIR * 128], BF16, kind="ExternalOutput").ap()
    dbg_ks = nc.dram_tensor("dbg_ks", [128, NPAIR * 2], BF16, kind="ExternalOutput").ap()
    dbg_den = nc.dram_tensor("dbg_den", [2, 512], F32, kind="ExternalOutput").ap()
    dbg_attn = nc.dram_tensor("dbg_attn", [128, NPAIR * TSEQ], BF16, kind="ExternalOutput").ap()
    dbg_gt = nc.dram_tensor("dbg_gt", [128, 4096], F32, kind="ExternalOutput").ap()
    wqkv_bf = nc.dram_tensor("wqkv_bf", [3 * D, D], BF16, kind="Internal").ap()
    wout_bf = nc.dram_tensor("wout_bf", [D, D], BF16, kind="Internal").ap()
    wg_bf = nc.dram_tensor("wg_bf", [16, D], BF16, kind="Internal").ap()
    xln_hbm = nc.dram_tensor("xln_hbm", [T, D], BF16, kind="Internal").ap()

    with ExitStack() as ctx:
        sbt = lambda name, shape, dt: ctx.enter_context(nc.sbuf_tensor(name, shape, dt)).ap()
        psf = lambda name, shape: ctx.enter_context(nc.psum_tensor(name, shape, F32)).ap()
        mksem = lambda name: ctx.enter_context(nc.semaphore(name))

        R = sbt("R", [128, NT * D], F32)
        arena = sbt("arena", [128, 32768], BF16)
        xlnT = arena[:, 0:16384].rearrange("p (c t) -> p c t", c=FC)
        kvmov = arena[:, 16384:32768].rearrange("p (c o) -> p c o", c=FC)
        attn_sc = arena[:, 16384:32768].rearrange("p (c t) -> p c t", c=NPAIR)
        woutmov = arena[:, 0:8192].rearrange("p (c o) -> p c o", c=FC)
        xg = arena[:, 8192:12288].rearrange("p (c t) -> p c t", c=FC)

        kbuf = sbt("kbuf", [128, D], BF16)
        vbuf = sbt("vbuf", [128, D], BF16)
        qbuf = sbt("qbuf", [128, 512], BF16)
        qstat = sbt("qstat", [128, 1, FC, 128], BF16)
        scrA = sbt("scrA", [128, 512], BF16)
        den_sb = sbt("den_sb", [2, 512], F32)
        rd_sb = sbt("rd_sb", [2, 512], BF16)
        scrB = sbt("scrB", [128, 512], BF16)
        s1s = sbt("s1s", [128, NT], F32)
        s2s = sbt("s2s", [128, NT], F32)
        mcol = sbt("mcol", [128, NT], F32)
        rcol = sbt("rcol", [128, NT], F32)
        kvblk = sbt("kvblk", [128, NPAIR, 128], BF16)
        kssta = sbt("kssta", [128, NPAIR, 2], BF16)
        onesf = sbt("onesf", [128, 1], F32)
        onesb = sbt("onesb", [128, 1], BF16)
        onesrow = sbt("onesrow", [1, 128], BF16)
        e2pair = sbt("e2pair", [2, 128], BF16)
        wgT = sbt("wgT", [128, FC, 16], BF16)
        hist_sb = sbt("hist_sb", [1, 2 * E], F32)

        p_a = psf("p_a", [128, 512])
        p_b = psf("p_b", [128, 512])
        p_c = psf("p_c", [128, 512])
        p_d = psf("p_d", [128, 512])
        p_kv = psf("p_kv", [128, 512])
        p_kv2 = psf("p_kv2", [128, 512])
        p_ks = psf("p_ks", [128, 8])
        p_h = psf("p_h", [128, 16])

        bout_s = qstat[0:1, 0, :, :].rearrange("p a b -> p (a b)")
        s_ld = mksem("s_ld")
        s_gd = mksem("s_gd")
        s_pe = mksem("s_pe")
        s_dv = mksem("s_dv")
        s_ac = mksem("s_ac")
        s_st = mksem("s_st")

        blk = ctx.enter_context(nc.Block())
        ctx.enter_context(nc.allow_low_precision(
            "bf16 staging of values that feed bf16 matmul operands"))
        em = Em()

        def _(g):
            em.fin(g.dma_start(out=wqkv_bf, in_=wqkv), s_gd, 16)
            em.fin(g.dma_start(out=wout_bf, in_=wout), s_gd, 16)
            em.fin(g.dma_start(out=wg_bf[0:E, :], in_=wg), s_gd, 16)
        blk.gpsimd(_)
        GD_W = em.cur(s_gd)

        def _(sy):
            em.fin(sy.dma_start(out=onesf, in_=c_ones128), s_ld, 16)
            em.fin(sy.dma_start(out=onesb, in_=c_onesb), s_ld, 16)
            em.fin(sy.dma_start(out=onesrow, in_=c_onesrow), s_ld, 16)
            em.fin(sy.dma_start(out=e2pair, in_=c_e2pair), s_ld, 16)
            em.fin(sy.dma_start(out=R.rearrange("p (n d) -> p n d", n=NT),
                                in_=x_in.rearrange("(n p) d -> p n d", p=128)),
                   s_ld, 16)
            sy.wait_ge(s_gd, GD_W)
            for fc in range(FC):
                em.fin(sy.dma_start(out=wgT[:, fc, :],
                                    in_=wg_bf[0:16, fc * 128:(fc + 1) * 128],
                                    transpose=True), s_ld, 16)
        blk.sync(_)
        LD0 = em.cur(s_ld)

        def emit_ln_stats(wait_dv=None):
            def _(v):
                v.wait_ge(s_ld, LD0)
                if wait_dv is not None:
                    v.wait_ge(s_dv, wait_dv)
                for t in range(NT - 1):
                    v.reduce_sum(s1s[:, t:t + 1], R[:, t * D:(t + 1) * D], axis=AX)
                em.fin(v.reduce_sum(s1s[:, NT - 1:NT], R[:, (NT - 1) * D:NT * D], axis=AX),
                       s_dv)
            blk.vector(_)

            def _(sc):
                sc.wait_ge(s_ld, LD0)
                if wait_dv is not None:
                    sc.wait_ge(s_dv, wait_dv)
                for t in range(NT - 1):
                    sc.activation(kbuf, R[:, t * D:(t + 1) * D], Act.Square,
                                  accum_out=s2s[:, t:t + 1])
                em.fin(sc.activation(kbuf, R[:, (NT - 1) * D:NT * D], Act.Square,
                                     accum_out=s2s[:, NT - 1:NT]), s_ac)
            blk.scalar(_)

            def _(v):
                v.wait_ge(s_ac, em.cur(s_ac))
                v.wait_ge(s_dv, em.cur(s_dv))
                v.tensor_scalar_mul(mcol, s1s, 1.0 / D)
                v.tensor_scalar_mul(rcol, s2s, 1.0 / D)
                v.tensor_tensor(s2s, mcol, mcol, Alu.mult)
                v.tensor_sub(rcol, rcol, s2s)
                em.fin(v.tensor_scalar_add(rcol, rcol, EPS_LN), s_dv)
            blk.vector(_)

            def _(sc):
                sc.wait_ge(s_dv, em.cur(s_dv))
                em.fin(sc.activation(s2s, rcol, Act.Sqrt), s_ac)
            blk.scalar(_)

            def _(v):
                v.wait_ge(s_ac, em.cur(s_ac))
                em.fin(v.reciprocal(rcol, s2s), s_dv)
            blk.vector(_)

        def emit_norm_store(dst_hbm):
            start_dv = em.cur(s_dv)
            norm_done, store_done = [], []
            for t in range(NT):
                def nrm(v, t=t):
                    if t == 0:
                        v.wait_ge(s_dv, start_dv)
                    if t >= 2:
                        v.wait_ge(s_gd, store_done[t - 2])
                    buf = kbuf if t % 2 == 0 else vbuf
                    for hh in range(2):
                        sl = slice(t * D + hh * 512, t * D + (hh + 1) * 512)
                        v.tensor_scalar_sub(scrA, R[:, sl], mcol[:, t:t + 1])
                        last = v.tensor_scalar_mul(buf[:, hh * 512:(hh + 1) * 512], scrA,
                                                   rcol[:, t:t + 1])
                    norm_done.append(em.fin(last, s_dv))
                blk.vector(nrm)

                def st(g, t=t):
                    g.wait_ge(s_dv, norm_done[t])
                    buf = kbuf if t % 2 == 0 else vbuf
                    store_done.append(em.fin(
                        g.dma_start(out=dst_hbm[t * 128:(t + 1) * 128, :], in_=buf[:, 0:D]),
                        s_gd, 16))
                blk.gpsimd(st)
            return store_done

        emit_ln_stats()
        emit_norm_store(xln_hbm)
        GD_XLN = em.cur(s_gd)

        def _(g):
            g.wait_ge(s_gd, em.cur(s_gd))
            em.fin(g.dma_start(out=dbg_xln, in_=xln_hbm), s_gd, 16)
        blk.gpsimd(_)

        for s in range(NSEQ):
            tok0 = s * TSEQ
            t0 = s * NTS

            DV_PREV = em.cur(s_dv)
            def _(sy, tok0=tok0, DV_PREV=DV_PREV):
                sy.wait_ge(s_gd, GD_XLN)
                sy.wait_ge(s_dv, DV_PREV)
                for fc in range(FC):
                    em.fin(sy.dma_start(out=xlnT[:, fc, :],
                                 in_=xln_hbm[tok0:tok0 + TSEQ, fc * 128:(fc + 1) * 128],
                                 transpose=True), s_ld, 16)
                for fc in range(FC):
                    em.fin(sy.dma_start(out=kvmov[:, fc, :],
                                        in_=wqkv_bf[D:3 * D, fc * 128:(fc + 1) * 128],
                                        transpose=True), s_ld, 16)
            blk.sync(_)
            LD_P1 = em.cur(s_ld)

            evac_done = []
            for tt in range(NTS):
                def pe1(t, tt=tt):
                    if tt == 0:
                        t.wait_ge(s_ld, LD_P1)
                    else:
                        t.wait_ge(s_dv, evac_done[tt - 1][1])
                    for fc in range(FC):
                        st_ap = xlnT[:, fc, tt * 128:(tt + 1) * 128]
                        for g, pp in enumerate((p_a, p_b, p_c, p_d)):
                            mm = t.matmul(pp, st_ap, kvmov[:, fc, g * 512:(g + 1) * 512],
                                          start=(fc == 0), stop=(fc == FC - 1))
                    em.fin(mm, s_pe)
                blk.tensor(pe1)
                PE_PROJ = em.cur(s_pe)

                def ev1(v, tt=tt):
                    v.wait_ge(s_pe, PE_PROJ)
                    em.fin(v.tensor_scalar_min(scrA, p_a, 0.0), s_dv)
                blk.vector(ev1)

                def ex1(sc, tt=tt):
                    sc.wait_ge(s_dv, em.cur(s_dv))
                    em.fin(sc.activation(scrB, scrA, Act.Exp), s_ac)
                blk.scalar(ex1)

                def st1(v, tt=tt):
                    v.wait_ge(s_ac, em.cur(s_ac))
                    v.scalar_tensor_tensor(kbuf[:, 0:512], p_a, 0.0, scrB, Alu.max, Alu.add)
                    em.fin(v.tensor_scalar_min(scrA, p_b, 0.0), s_dv)
                blk.vector(st1)

                def ex2(sc, tt=tt):
                    sc.wait_ge(s_dv, em.cur(s_dv))
                    em.fin(sc.activation(scrB, scrA, Act.Exp), s_ac)
                blk.scalar(ex2)

                def st2(v, tt=tt):
                    v.wait_ge(s_ac, em.cur(s_ac))
                    v.scalar_tensor_tensor(kbuf[:, 512:1024], p_b, 0.0, scrB, Alu.max, Alu.add)
                    v.tensor_copy(vbuf[:, 0:512], p_c)
                    evac_done.append((None, em.fin(v.tensor_copy(vbuf[:, 512:1024], p_d), s_dv)))
                blk.vector(st2)
                DV_EVAC = em.cur(s_dv)

                def pe2(t, tt=tt):
                    t.wait_ge(s_dv, DV_EVAC)
                    for pr in range(NPAIR):
                        pk = p_kv if pr < 4 else p_kv2
                        col = (pr % 4) * 128
                        t.matmul(pk[:, col:col + 128], kbuf[:, pr * 128:(pr + 1) * 128],
                                 vbuf[:, pr * 128:(pr + 1) * 128],
                                 start=(tt == 0), stop=(tt == NTS - 1))
                        mm = t.matmul(p_ks[:, pr:pr + 1], kbuf[:, pr * 128:(pr + 1) * 128],
                                      onesb, start=(tt == 0), stop=(tt == NTS - 1))
                    em.fin(mm, s_pe)
                blk.tensor(pe2)
            PE_KV = em.cur(s_pe)

            def _(v):
                v.wait_ge(s_pe, PE_KV)
                v.memset(kvblk, 0.0)
                v.memset(kssta, 0.0)
                for pr in range(NPAIR):
                    pk = p_kv if pr < 4 else p_kv2
                    col = (pr % 4) * 128
                    v.tensor_copy(kvblk[0:64, pr, 0:64], pk[0:64, col:col + 64])
                    v.tensor_copy(kvblk[64:128, pr, 64:128], pk[64:128, col + 64:col + 128])
                    v.tensor_copy(kssta[0:64, pr, 0:1], p_ks[0:64, pr:pr + 1])
                    em.fin(v.tensor_copy(kssta[64:128, pr, 1:2], p_ks[64:128, pr:pr + 1]), s_dv)
            blk.vector(_)
            DV_KVBLK = em.cur(s_dv)

            if s == 0:
                def _(g):
                    g.wait_ge(s_dv, DV_KVBLK)
                    em.fin(g.dma_start(out=dbg_kvblk,
                                in_=kvblk.rearrange("p a b -> p (a b)")), s_gd, 16)
                    em.fin(g.dma_start(out=dbg_ks,
                                in_=kssta.rearrange("p a b -> p (a b)")), s_gd, 16)
                blk.gpsimd(_)

            qld = []
            for pr in range(NPAIR):
                def qload(sy, pr=pr):
                    if pr >= 1:
                        sy.wait_ge(s_pe, qld[pr - 1][1])
                    last = 0
                    for fc in range(FC):
                        last = em.fin(sy.dma_start(out=qstat[:, 0, fc, :],
                                          in_=wqkv_bf[pr * 128:(pr + 1) * 128,
                                                      fc * 128:(fc + 1) * 128],
                                          transpose=True), s_ld, 16)
                    qld.append((last, None))
                blk.sync(qload)
                LD_Q = em.cur(s_ld)

                last_pe_for_qstat = None
                for g in range(4):
                    def qproj(t, pr=pr, g=g):
                        if g == 0:
                            t.wait_ge(s_ld, LD_Q)
                            t.wait_ge(s_dv, DV_KVBLK)
                        if pr > 0 or g > 0:
                            t.wait_ge(s_dv, em.cur(s_dv))
                        for fc in range(FC):
                            mm = t.matmul(p_a, qstat[:, 0, fc, :],
                                          xlnT[:, fc, g * 512:(g + 1) * 512],
                                          start=(fc == 0), stop=(fc == FC - 1))
                        em.fin(mm, s_pe)
                    blk.tensor(qproj)

                    def qev(v, pr=pr, g=g):
                        v.wait_ge(s_pe, em.cur(s_pe))
                        em.fin(v.tensor_scalar_min(scrA, p_a, 0.0), s_dv)
                    blk.vector(qev)

                    def qex(sc, pr=pr, g=g):
                        sc.wait_ge(s_dv, em.cur(s_dv))
                        em.fin(sc.activation(scrB, scrA, Act.Exp), s_ac)
                    blk.scalar(qex)

                    def qst(v, pr=pr, g=g):
                        v.wait_ge(s_ac, em.cur(s_ac))
                        em.fin(v.scalar_tensor_tensor(qbuf, p_a, 0.0, scrB,
                                                      Alu.max, Alu.add), s_dv)
                    blk.vector(qst)
                    DV_Q = em.cur(s_dv)

                    def amm(t, pr=pr, g=g):
                        t.wait_ge(s_dv, DV_Q)
                        t.matmul(p_b, kvblk[:, pr, :], qbuf, start=True, stop=True)
                        em.fin(t.matmul(p_c[0:2, :], kssta[:, pr, :], qbuf,
                                        start=True, stop=True), s_pe)
                    blk.tensor(amm)
                    PE_ATT = em.cur(s_pe)

                    def dev(v, pr=pr, g=g):
                        v.wait_ge(s_pe, PE_ATT)
                        v.tensor_scalar_add(den_sb, p_c[0:2, :], EPS_ATTN)
                        em.fin(v.reciprocal(rd_sb, den_sb), s_dv)
                    blk.vector(dev)
                    DV_RD = em.cur(s_dv)

                    if s == 0 and pr == 0 and g == 0:
                        def dend(gg, DV_RD=DV_RD):
                            gg.wait_ge(s_dv, DV_RD)
                            em.fin(gg.dma_start(out=dbg_den, in_=den_sb), s_gd, 16)
                        blk.gpsimd(dend)

                    def rmm(t, pr=pr, g=g):
                        t.wait_ge(s_dv, DV_RD)
                        em.fin(t.matmul(p_d, e2pair, rd_sb, start=True, stop=True), s_pe)
                    blk.tensor(rmm)
                    PE_RDB = em.cur(s_pe)

                    def asc(v, pr=pr, g=g):
                        v.wait_ge(s_pe, PE_RDB)
                        v.tensor_copy(scrA, p_d)
                        em.fin(v.tensor_tensor(attn_sc[:, pr, g * 512:(g + 1) * 512],
                                               p_b, scrA, Alu.mult), s_dv)
                    blk.vector(asc)
                qld[pr] = (qld[pr][0], em.cur(s_pe))
            DV_ATTN = em.cur(s_dv)

            if s == 0:
                def _(g):
                    g.wait_ge(s_dv, DV_ATTN)
                    em.fin(g.dma_start(out=dbg_attn,
                                in_=attn_sc.rearrange("p a b -> p (a b)")), s_gd, 16)
                blk.gpsimd(_)

            def _(g):
                g.wait_ge(s_pe, em.cur(s_pe))
                em.fin(g.dma_start(out=bout_s, in_=bout), s_gd, 16)
            blk.gpsimd(_)
            GD_BOUT = em.cur(s_gd)

            def _(sy):
                sy.wait_ge(s_dv, DV_ATTN)
                for ec in range(FC):
                    em.fin(sy.dma_start(out=woutmov[:, ec, :],
                                      in_=wout_bf[0:D, ec * 128:(ec + 1) * 128],
                                      transpose=True), s_ld, 16)
            blk.sync(_)
            LD_WO = em.cur(s_ld)

            for tt in range(NTS):
                def opj(t, tt=tt):
                    if tt == 0:
                        t.wait_ge(s_ld, LD_WO)
                        t.wait_ge(s_gd, GD_BOUT)
                    else:
                        t.wait_ge(s_dv, em.cur(s_dv))
                    for half, pp in enumerate((p_a, p_b)):
                        for ec in range(FC):
                            t.matmul(pp, attn_sc[:, ec, tt * 128:(tt + 1) * 128],
                                     woutmov[:, ec, half * 512:(half + 1) * 512],
                                     start=(ec == 0), stop=False)
                        mm = t.matmul(pp, onesrow,
                                      bout_s[0:1, half * 512:(half + 1) * 512],
                                      start=False, stop=True)
                    em.fin(mm, s_pe)
                blk.tensor(opj)
                PE_OP = em.cur(s_pe)

                def radd(v, tt=tt, t0=t0):
                    v.wait_ge(s_pe, PE_OP)
                    base = (t0 + tt) * D
                    v.tensor_add(R[:, base:base + 512], R[:, base:base + 512], p_a)
                    em.fin(v.tensor_add(R[:, base + 512:base + 1024],
                                        R[:, base + 512:base + 1024], p_b), s_dv)
                blk.vector(radd)

        emit_ln_stats(wait_dv=em.cur(s_dv))
        xln1_done = emit_norm_store(xln_hbm)
        GD_XLN1 = em.cur(s_gd)

        def _(g):
            g.wait_ge(s_gd, GD_XLN1)
            em.fin(g.dma_start(out=xln1_out, in_=xln_hbm), s_gd, 16)
            em.fin(g.dma_start(out=x1_out.rearrange("(n p) d -> p n d", p=128),
                                in_=R.rearrange("p (n d) -> p n d", n=NT)),
                   s_gd, 16)
        blk.gpsimd(_)

        rbuf = arena[:, 24576:32768].bitcast(F32)
        G0, W1c, W2c, OH1, OH2, MXc, R2c, MSK = 0, 256, 512, 768, 1024, 1280, 1312, 1344

        for grp in range(8):
            def gld(sy, grp=grp):
                sy.wait_ge(s_gd, GD_XLN1)
                if grp > 0:
                    sy.wait_ge(s_pe, em.cur(s_pe))
                for fc in range(FC):
                    em.fin(sy.dma_start(out=xg[:, fc, :],
                                      in_=xln_hbm[grp * 512:(grp + 1) * 512,
                                                  fc * 128:(fc + 1) * 128],
                                      transpose=True), s_ld, 16)
            blk.sync(gld)
            LD_XG = em.cur(s_ld)

            def lmm(t, grp=grp, LD_XG=LD_XG):
                t.wait_ge(s_ld, LD_XG)
                for t4 in range(4):
                    tg = grp * 4 + t4
                    for fc in range(FC):
                        mm = t.matmul(p_a[:, tg * 8:tg * 8 + 8],
                                      xg[:, fc, t4 * 128:(t4 + 1) * 128],
                                      wgT[:, fc, 0:8], start=(fc == 0), stop=(fc == FC - 1))
                em.fin(mm, s_pe)
            blk.tensor(lmm)
        PE_LG = em.cur(s_pe)

        def rt(v, instr):
            em.fin(instr, s_dv)
            v.wait_ge(s_dv, em.cur(s_dv))

        l3 = lambda base: rbuf[:, base:base + 256].rearrange("p (t e) -> p t e", e=8)

        def _(v):
            v.wait_ge(s_pe, PE_LG)
            rt(v, v.tensor_copy(rbuf[:, W1c:W1c + 256], p_a[:, 0:256]))
            rt(v, v.reduce_max(rbuf[:, MXc:MXc + 32], l3(W1c), axis=AX))
            rt(v, v.tensor_sub(l3(W2c), l3(W1c),
                               rbuf[:, MXc:MXc + 32][:, :, None].broadcast_to([128, 32, 8])))
        blk.vector(_)

        def _(sc):
            sc.wait_ge(s_dv, em.cur(s_dv))
            em.fin(sc.activation(rbuf[:, G0:G0 + 256], rbuf[:, W2c:W2c + 256], Act.Exp), s_ac)
        blk.scalar(_)

        def _(v):
            v.wait_ge(s_ac, em.cur(s_ac))
            rt(v, v.tensor_scalar_mul(rbuf[:, W1c:W1c + 256], rbuf[:, G0:G0 + 256], 2097152.0))
            rt(v, v.tensor_scalar_add(rbuf[:, W2c:W2c + 256], rbuf[:, W1c:W1c + 256], -2097151.0))
            rt(v, v.tensor_relu(rbuf[:, W1c:W1c + 256], rbuf[:, W2c:W2c + 256]))
            rt(v, v.tensor_scalar_mul(rbuf[:, W2c:W2c + 256], rbuf[:, W1c:W1c + 256], 8.0))
            rt(v, v.tensor_scalar_add(rbuf[:, W1c:W1c + 256], rbuf[:, W2c:W2c + 256], -5.0))
            rt(v, v.tensor_relu(rbuf[:, W2c:W2c + 256], rbuf[:, W1c:W1c + 256]))
            rt(v, v.tensor_scalar_min(rbuf[:, W1c:W1c + 256], rbuf[:, W2c:W2c + 256], 1.0))
            for st_, (src_, dst_) in zip((1, 2, 4), ((W1c, W2c), (W2c, MSK), (MSK, W2c))):
                rt(v, v.tensor_copy(rbuf[:, dst_:dst_ + 256].rearrange("p (t e) -> p t e", e=8)[:, :, 0:st_],
                                    l3(src_)[:, :, 0:st_]))
                rt(v, v.tensor_max(l3(dst_)[:, :, st_:8], l3(src_)[:, :, st_:8],
                                   l3(src_)[:, :, 0:8 - st_]))
            rt(v, v.tensor_scalar_mul(rbuf[:, MSK:MSK + 256], rbuf[:, W2c:W2c + 256], -1.0))
            rt(v, v.tensor_scalar_add(rbuf[:, W2c:W2c + 256], rbuf[:, MSK:MSK + 256], 1.0))
            rt(v, v.tensor_copy(l3(OH1)[:, :, 0:1], l3(W1c)[:, :, 0:1]))
            rt(v, v.tensor_mul(l3(OH1)[:, :, 1:8], l3(W1c)[:, :, 1:8], l3(W2c)[:, :, 0:7]))
            rt(v, v.scalar_tensor_tensor(rbuf[:, MSK:MSK + 256], rbuf[:, OH1:OH1 + 256], -2.0,
                                         rbuf[:, G0:G0 + 256], Alu.mult, Alu.add))
            rt(v, v.reduce_max(rbuf[:, MXc:MXc + 32], l3(MSK), axis=AX))
            rt(v, v.reciprocal(rbuf[:, R2c:R2c + 32], rbuf[:, MXc:MXc + 32]))
            rt(v, v.tensor_sub(l3(W1c), rbuf[:, MXc:MXc + 32][:, :, None].broadcast_to([128, 32, 8]),
                               l3(MSK)))
            rt(v, v.tensor_scalar_mul(rbuf[:, W2c:W2c + 32], rbuf[:, R2c:R2c + 32], 16777216.0))
            rt(v, v.tensor_mul(l3(MSK), l3(W1c),
                               rbuf[:, W2c:W2c + 32][:, :, None].broadcast_to([128, 32, 8])))
            rt(v, v.tensor_scalar_mul(rbuf[:, W1c:W1c + 256], rbuf[:, MSK:MSK + 256], -4.0))
            rt(v, v.tensor_scalar_add(rbuf[:, W2c:W2c + 256], rbuf[:, W1c:W1c + 256], 1.0))
            rt(v, v.tensor_relu(rbuf[:, W1c:W1c + 256], rbuf[:, W2c:W2c + 256]))
            rt(v, v.tensor_scalar_mul(rbuf[:, MSK:MSK + 256], rbuf[:, OH1:OH1 + 256], -1.0))
            rt(v, v.tensor_scalar_add(rbuf[:, W2c:W2c + 256], rbuf[:, MSK:MSK + 256], 1.0))
            rt(v, v.tensor_mul(rbuf[:, MSK:MSK + 256], rbuf[:, W1c:W1c + 256], rbuf[:, W2c:W2c + 256]))
            for st_, (src_, dst_) in zip((1, 2, 4), ((MSK, W1c), (W1c, W2c), (W2c, W1c))):
                rt(v, v.tensor_copy(rbuf[:, dst_:dst_ + 256].rearrange("p (t e) -> p t e", e=8)[:, :, 0:st_],
                                    l3(src_)[:, :, 0:st_]))
                rt(v, v.tensor_max(l3(dst_)[:, :, st_:8], l3(src_)[:, :, st_:8],
                                   l3(src_)[:, :, 0:8 - st_]))
            rt(v, v.tensor_scalar_mul(rbuf[:, W2c:W2c + 256], rbuf[:, W1c:W1c + 256], -1.0))
            rt(v, v.tensor_scalar_add(rbuf[:, W1c:W1c + 256], rbuf[:, W2c:W2c + 256], 1.0))
            rt(v, v.tensor_copy(l3(OH2)[:, :, 0:1], l3(MSK)[:, :, 0:1]))
            rt(v, v.tensor_mul(l3(OH2)[:, :, 1:8], l3(MSK)[:, :, 1:8], l3(W1c)[:, :, 0:7]))
        blk.vector(_)
        DV_OH = em.cur(s_dv)

        def _(g):
            g.wait_ge(s_dv, DV_OH)
            em.fin(g.dma_start(out=dbg_gt, in_=rbuf), s_gd, 16)
        blk.gpsimd(_)

        def _(v):
            rt(v, v.reduce_sum(rbuf[:, W1c:W1c + 8], l3(OH1).rearrange("p t e -> p e t"),
                               axis=AX))
            rt(v, v.reduce_sum(rbuf[:, W1c + 8:W1c + 16],
                               l3(OH2).rearrange("p t e -> p e t"), axis=AX))
        blk.vector(_)
        DV_HP = em.cur(s_dv)

        def _(t):
            t.wait_ge(s_dv, DV_HP)
            em.fin(t.matmul(p_h[0:1, 0:16], onesf, rbuf[:, W1c:W1c + 16],
                            start=True, stop=True), s_pe)
        blk.tensor(_)

        PE_H = em.cur(s_pe)

        def _(v):
            v.wait_ge(s_pe, PE_H)
            em.fin(v.tensor_copy(hist_sb, p_h[0:1, 0:16]), s_dv)
        blk.vector(_)

        def _(sy):
            sy.wait_ge(s_dv, em.cur(s_dv))
            em.fin(sy.dma_start(out=hist_out.rearrange("a b -> (a b)")[None, :], in_=hist_sb),
                   s_st, 16)
            sy.wait_ge(s_st, em.cur(s_st))
            sy.wait_ge(s_gd, em.cur(s_gd))
        blk.sync(_)

    return nc





NBUF = E * C_LOC + 128
SG = 512
NSG = C_LOC // SG


def build_launch_b(cut=None):
    nc = bass.Bass("TRN2", target_bir_lowering=False)

    x1_in = nc.dram_tensor("x1_in", [T, D], F32, kind="ExternalInput").ap()
    xln1_in = nc.dram_tensor("xln1_in", [T, D], BF16, kind="ExternalInput").ap()
    hist_all = nc.dram_tensor("hist_all", [8, 2 * E], F32, kind="ExternalInput").ap()
    hist_prev = nc.dram_tensor("hist_prev", [8, 2 * E], F32, kind="ExternalInput").ap()
    wg = nc.dram_tensor("wg", [E, D], F32, kind="ExternalInput").ap()
    w1_in = nc.dram_tensor("w1_in", [E, D, HID], F32, kind="ExternalInput").ap()
    b1_in = nc.dram_tensor("b1_in", [E, HID], F32, kind="ExternalInput").ap()
    w2_in = nc.dram_tensor("w2_in", [E, HID, D], F32, kind="ExternalInput").ap()
    b2_in = nc.dram_tensor("b2_in", [E, D], F32, kind="ExternalInput").ap()
    wq_in = nc.dram_tensor("wq_in", [D, D], F32, kind="ExternalInput").ap()
    wk_in = nc.dram_tensor("wk_in", [D, D], F32, kind="ExternalInput").ap()
    wv_in = nc.dram_tensor("wv_in", [D, D], F32, kind="ExternalInput").ap()
    wo_in = nc.dram_tensor("wo_in", [D, D], F32, kind="ExternalInput").ap()
    wgam = nc.dram_tensor("wgam", [1, 16], F32, kind="ExternalInput").ap()
    bgam = nc.dram_tensor("bgam", [1, 16], F32, kind="ExternalInput").ap()
    wlam = nc.dram_tensor("wlam", [1, 16], F32, kind="ExternalInput").ap()
    blam = nc.dram_tensor("blam", [1, 16], F32, kind="ExternalInput").ap()
    wr_in = nc.dram_tensor("wr_in", [D, D], F32, kind="ExternalInput").ap()
    wk2_in = nc.dram_tensor("wk2_in", [D, D], F32, kind="ExternalInput").ap()
    wv2_in = nc.dram_tensor("wv2_in", [D, D], F32, kind="ExternalInput").ap()
    mur_in = nc.dram_tensor("mur_in", [1, D], F32, kind="ExternalInput").ap()
    muk_in = nc.dram_tensor("muk_in", [1, D], F32, kind="ExternalInput").ap()
    c_ones128 = nc.dram_tensor("c_ones128", [128, 1], F32, kind="ExternalInput").ap()
    c_onesb = nc.dram_tensor("c_onesb", [128, 1], BF16, kind="ExternalInput").ap()
    c_onesrow = nc.dram_tensor("c_onesrow", [1, 128], BF16, kind="ExternalInput").ap()
    c_onesrowf = nc.dram_tensor("c_onesrowf", [1, 128], F32, kind="ExternalInput").ap()
    c_trils = nc.dram_tensor("c_trils", [128, 128], F32, kind="ExternalInput").ap()
    c_identf = nc.dram_tensor("c_identf", [128, 128], F32, kind="ExternalInput").ap()
    c_trash = nc.dram_tensor("c_trash", [128, 1], F32, kind="ExternalInput").ap()
    c_eclrow = nc.dram_tensor("c_eclrow", [1, 8], F32, kind="ExternalInput").ap()
    c_eclbc = nc.dram_tensor("c_eclbc", [128, 8], F32, kind="ExternalInput").ap()
    c_onesrow5 = nc.dram_tensor("c_onesrow5", [1, 512], BF16, kind="ExternalInput").ap()
    c_negmask4 = nc.dram_tensor("c_negmask4", [128, 512], F32, kind="ExternalInput").ap()
    c_identb = nc.dram_tensor("c_identb", [128, 128], BF16, kind="ExternalInput").ap()
    c_selg = nc.dram_tensor("c_selg", [16, 2048], F32, kind="ExternalInput").ap()
    c_parw = nc.dram_tensor("c_parw", [16, 128], F32, kind="ExternalInput").ap()
    c_selpair = nc.dram_tensor("c_selpair", [16, 512], F32, kind="ExternalInput").ap()

    out_f = nc.dram_tensor("out_f", [T, D], F32, kind="ExternalOutput").ap()
    dbg_b = nc.dram_tensor("dbg_b", [128, 4096], F32, kind="ExternalOutput").ap()

    wg_bf = nc.dram_tensor("wg_bf", [16, D], BF16, kind="Internal").ap()
    buf_h = nc.dram_tensor("buf_h", [NBUF, D], BF16, kind="Internal").ap()
    ybuf_h = nc.dram_tensor("ybuf_h", [NBUF, D], BF16, kind="Internal").ap()
    xln_hbm = nc.dram_tensor("xln_hbm", [T, D], BF16, kind="Internal").ap()
    wq_bf = nc.dram_tensor("wq_bf", [D, D], BF16, kind="Internal").ap()
    wk_bf = nc.dram_tensor("wk_bf", [D, D], BF16, kind="Internal").ap()
    wv_bf = nc.dram_tensor("wv_bf", [D, D], BF16, kind="Internal").ap()
    wo_bf = nc.dram_tensor("wo_bf", [D, D], BF16, kind="Internal").ap()
    wr_bf = nc.dram_tensor("wr_bf", [D, D], BF16, kind="Internal").ap()
    wk2_bf = nc.dram_tensor("wk2_bf", [D, D], BF16, kind="Internal").ap()
    wv2_bf = nc.dram_tensor("wv2_bf", [D, D], BF16, kind="Internal").ap()
    x2_h = nc.dram_tensor("x2_h", [T, D], F32, kind="Internal").ap()
    xln3_h = nc.dram_tensor("xln3_h", [T, D], BF16, kind="Internal").ap()

    with ExitStack() as ctx:
        sbt = lambda name, shape, dt: ctx.enter_context(nc.sbuf_tensor(name, shape, dt)).ap()
        psf = lambda name, shape: ctx.enter_context(nc.psum_tensor(name, shape, F32)).ap()
        mksem = lambda name: ctx.enter_context(nc.semaphore(name))

        R = sbt("R", [128, NT * D], F32)
        arena = sbt("arena", [128, 34880], BF16)
        xg = arena[:, 0:4096].rearrange("p (c t) -> p c t", c=FC)
        rbuf = arena[:, 4096:20480].bitcast(F32)
        bufT = arena[:, 21504:25600].rearrange("p (c s) -> p c s", c=FC)
        h_s = arena[:, 5120:21504].rearrange("p (c s) -> p c s", c=32)
        xlnT = arena[:, 0:16384].rearrange("p (c t) -> p c t", c=FC)
        ybufT = arena[:, 16384:32768].rearrange("p (c t) -> p c t", c=FC)
        lf32 = arena[:, 16384:34880].bitcast(F32)
        negmask4 = lf32[:, 8224:8736]
        S_bf = arena[:, 33856:34368].rearrange("p (g k) -> p g k", g=8)

        kbuf = sbt("kbuf", [128, D], BF16)
        vbuf = sbt("vbuf", [128, D], BF16)
        wstat = sbt("wstat", [128, 1, FC, 128], BF16)
        wmov = sbt("wmov", [128, 1, 1024], BF16)
        s1s = sbt("s1s", [128, NT], F32)
        s2s = sbt("s2s", [128, NT], F32)
        mcol = sbt("mcol", [128, NT], F32)
        rcol = sbt("rcol", [128, NT], F32)
        slot1 = sbt("slot1", [128, NT], U32)
        slot2 = sbt("slot2", [128, NT], U32)
        cw1 = sbt("cw1", [128, NT], F32)
        cw2 = sbt("cw2", [128, NT], F32)
        onesf = sbt("onesf", [128, 1], F32)
        onesb = sbt("onesb", [128, 1], BF16)
        onesrow = sbt("onesrow", [1, 128], BF16)
        onesrowf = sbt("onesrowf", [1, 128], F32)
        trils = sbt("trils", [128, 128], F32)
        identf = sbt("identf", [128, 128], F32)
        trash = sbt("trash", [128, 1], F32)
        wgT = sbt("wgT", [128, FC, 16], BF16)
        hrow = sbt("hrow", [32, 112], F32)
        crow = None
        b1r = arena[0:1, 0:4096]
        b2r = arena[0:1, 4096:5120]

        p_a = psf("p_a", [128, 512])
        p_b = psf("p_b", [128, 512])
        p_c = psf("p_c", [128, 512])
        p_d = psf("p_d", [128, 512])
        p_e = psf("p_e", [128, 512])
        p_f = psf("p_f", [128, 512])
        p_g = psf("p_g", [128, 512])
        p_h = psf("p_h", [128, 512])
        PS = [p_a, p_b, p_c, p_d, p_e, p_f, p_g, p_h]

        s_ld = mksem("s_ld")
        s_gd = mksem("s_gd")
        s_pe = mksem("s_pe")
        s_dv = mksem("s_dv")
        s_ac = mksem("s_ac")
        s_st = mksem("s_st")

        crow = rbuf[0:1, 4096:5120]
        eclbc = rbuf[:, 5120:5128]
        onesrow5 = arena[0:1, 34368:34880]

        blk = ctx.enter_context(nc.Block())
        ctx.enter_context(nc.allow_low_precision("bf16 staging for bf16 matmul operands"))
        em = Em()

        def _(g):
            for wsrc, wdst in ((wg, wg_bf[0:E, :]), (wq_in, wq_bf), (wk_in, wk_bf),
                               (wv_in, wv_bf), (wo_in, wo_bf), (wr_in, wr_bf),
                               (wk2_in, wk2_bf), (wv2_in, wv2_bf)):
                em.fin(g.dma_start(out=wdst, in_=wsrc), s_gd, 16)
        blk.gpsimd(_)
        GD_W = em.cur(s_gd)

        def _(sy):
            for dst, src_ in ((onesf, c_ones128), (onesb, c_onesb), (onesrow, c_onesrow),
                              (onesrowf, c_onesrowf), (trils, c_trils), (identf, c_identf),
                              (trash, c_trash), (crow[0:1, 40:48], c_eclrow),
                              (eclbc, c_eclbc),
                              (hrow[0:8, 0:16], hist_all.rearrange("a b -> a b")[0:8, :]),
                              (hrow[0:8, 16:32], hist_prev[0:8, :])):
                em.fin(sy.dma_start(out=dst, in_=src_), s_ld, 16)
            sy.wait_ge(s_gd, GD_W)
            for fc in range(FC):
                em.fin(sy.dma_start(out=wgT[:, fc, :], in_=wg_bf[0:16, fc * 128:(fc + 1) * 128],
                                    transpose=True), s_ld, 16)
        blk.sync(_)
        LD0 = em.cur(s_ld)

        def _(v):
            em.fin(v.memset(kbuf, 0.0), s_dv)
        blk.vector(_)

        def _(g):
            g.wait_ge(s_dv, em.cur(s_dv))
            em.fin(g.dma_start(out=ybuf_h[E * C_LOC:E * C_LOC + 128, :], in_=kbuf), s_gd, 16)
        blk.gpsimd(_)

        def _(t):
            t.wait_ge(s_ld, LD0)
            t.matmul(p_a[0:1, 0:16], onesf[0:8, 0:1], hrow[0:8, 0:16], start=True, stop=True)
            em.fin(t.matmul(p_a[0:1, 16:32], onesf[0:8, 0:1], hrow[0:8, 16:32],
                            start=True, stop=True), s_pe)
        blk.tensor(_)

        def _(v):
            v.wait_ge(s_pe, em.cur(s_pe))
            em.fin(v.tensor_copy(crow[0:1, 0:32], p_a[0:1, 0:32]), s_dv)
            v.wait_ge(s_dv, em.cur(s_dv))
            em.fin(v.tensor_add(crow[0:1, 32:40], crow[0:1, 0:8], crow[0:1, 24:32]), s_dv)
        blk.vector(_)
        DV_BASE = em.cur(s_dv)

        def _(t):
            t.wait_ge(s_dv, DV_BASE)
            t.transpose(p_b[0:8, 0:1], crow[0:1, 16:24], identf[0:1, 0:1])
            em.fin(t.transpose(p_b[0:8, 8:9], crow[0:1, 32:40], identf[0:1, 0:1]), s_pe)
        blk.tensor(_)

        def _(v):
            v.wait_ge(s_pe, em.cur(s_pe))
            v.tensor_copy(hrow[0:8, 31:32], p_b[0:8, 0:1])
            em.fin(v.tensor_copy(hrow[0:8, 30:31], p_b[0:8, 8:9]), s_dv)
        blk.vector(_)
        DV_BASE3 = em.cur(s_dv)

        G0, W1c, W2c, OH1, OH2, MXc, R2c, MSK = 0, 256, 512, 768, 1024, 1280, 1312, 1344
        SL1, SL2, CW1c, CW2c, PK1, PK2 = 1600, 1632, 1664, 1696, 1728, 1984
        B1B, B2B, C1B, C2B = 2240, 2496, 2752, 3008
        RSU = 3264

        l3 = lambda base: rbuf[:, base:base + 256].rearrange("p (t e) -> p t e", e=8)

        for grp in range(8):
            def gld(sy, grp=grp):
                sy.wait_ge(s_ld, LD0)
                if grp > 0:
                    sy.wait_ge(s_pe, em.cur(s_pe))
                for fc in range(FC):
                    em.fin(sy.dma_start(out=xg[:, fc, :],
                                      in_=xln1_in[grp * 512:(grp + 1) * 512,
                                                  fc * 128:(fc + 1) * 128],
                                      transpose=True), s_ld, 16)
            blk.sync(gld)
            LD_XG = em.cur(s_ld)

            def lmm(t, grp=grp, LD_XG=LD_XG):
                t.wait_ge(s_ld, LD_XG)
                for t4 in range(4):
                    tg = grp * 4 + t4
                    for fc in range(FC):
                        mm = t.matmul(p_a[:, tg * 8:tg * 8 + 8],
                                      xg[:, fc, t4 * 128:(t4 + 1) * 128],
                                      wgT[:, fc, 0:8], start=(fc == 0), stop=(fc == FC - 1))
                em.fin(mm, s_pe)
            blk.tensor(lmm)
        PE_LG = em.cur(s_pe)

        def rt(v, instr):
            em.fin(instr, s_dv)
            v.wait_ge(s_dv, em.cur(s_dv))

        def _(v):
            v.wait_ge(s_pe, PE_LG)
            rt(v, v.tensor_copy(rbuf[:, W1c:W1c + 256], p_a[:, 0:256]))
            rt(v, v.reduce_max(rbuf[:, MXc:MXc + 32], l3(W1c), axis=AX))
            rt(v, v.tensor_sub(l3(W2c), l3(W1c),
                               rbuf[:, MXc:MXc + 32][:, :, None].broadcast_to([128, 32, 8])))
        blk.vector(_)

        def _(sc):
            sc.wait_ge(s_dv, em.cur(s_dv))
            em.fin(sc.activation(rbuf[:, G0:G0 + 256], rbuf[:, W2c:W2c + 256], Act.Exp), s_ac)
        blk.scalar(_)

        def _(v):
            v.wait_ge(s_ac, em.cur(s_ac))
            rt(v, v.reduce_sum(rbuf[:, MXc:MXc + 32], l3(G0), axis=AX))
            rt(v, v.reciprocal(rbuf[:, RSU:RSU + 32], rbuf[:, MXc:MXc + 32]))
            rt(v, v.tensor_scalar_mul(rbuf[:, W1c:W1c + 256], rbuf[:, G0:G0 + 256], 2097152.0))
            rt(v, v.tensor_scalar_add(rbuf[:, W2c:W2c + 256], rbuf[:, W1c:W1c + 256], -2097151.0))
            rt(v, v.tensor_relu(rbuf[:, W1c:W1c + 256], rbuf[:, W2c:W2c + 256]))
            rt(v, v.tensor_scalar_mul(rbuf[:, W2c:W2c + 256], rbuf[:, W1c:W1c + 256], 8.0))
            rt(v, v.tensor_scalar_add(rbuf[:, W1c:W1c + 256], rbuf[:, W2c:W2c + 256], -5.0))
            rt(v, v.tensor_relu(rbuf[:, W2c:W2c + 256], rbuf[:, W1c:W1c + 256]))
            rt(v, v.tensor_scalar_min(rbuf[:, W1c:W1c + 256], rbuf[:, W2c:W2c + 256], 1.0))
            for st_, (src_, dst_) in zip((1, 2, 4), ((W1c, W2c), (W2c, MSK), (MSK, W2c))):
                rt(v, v.tensor_copy(rbuf[:, dst_:dst_ + 256].rearrange("p (t e) -> p t e", e=8)[:, :, 0:st_],
                                    l3(src_)[:, :, 0:st_]))
                rt(v, v.tensor_max(l3(dst_)[:, :, st_:8], l3(src_)[:, :, st_:8],
                                   l3(src_)[:, :, 0:8 - st_]))
            rt(v, v.tensor_scalar_mul(rbuf[:, MSK:MSK + 256], rbuf[:, W2c:W2c + 256], -1.0))
            rt(v, v.tensor_scalar_add(rbuf[:, W2c:W2c + 256], rbuf[:, MSK:MSK + 256], 1.0))
            rt(v, v.tensor_copy(l3(OH1)[:, :, 0:1], l3(W1c)[:, :, 0:1]))
            rt(v, v.tensor_mul(l3(OH1)[:, :, 1:8], l3(W1c)[:, :, 1:8], l3(W2c)[:, :, 0:7]))
            rt(v, v.scalar_tensor_tensor(rbuf[:, MSK:MSK + 256], rbuf[:, OH1:OH1 + 256], -2.0,
                                         rbuf[:, G0:G0 + 256], Alu.mult, Alu.add))
            rt(v, v.reduce_max(rbuf[:, MXc:MXc + 32], l3(MSK), axis=AX))
            rt(v, v.reciprocal(rbuf[:, R2c:R2c + 32], rbuf[:, MXc:MXc + 32]))
            rt(v, v.tensor_copy(rbuf[:, CW1c:CW1c + 32], rbuf[:, RSU:RSU + 32]))
            rt(v, v.tensor_mul(rbuf[:, CW2c:CW2c + 32], rbuf[:, MXc:MXc + 32],
                               rbuf[:, RSU:RSU + 32]))
            rt(v, v.tensor_sub(l3(W1c), rbuf[:, MXc:MXc + 32][:, :, None].broadcast_to([128, 32, 8]),
                               l3(MSK)))
            rt(v, v.tensor_scalar_mul(rbuf[:, W2c:W2c + 32], rbuf[:, R2c:R2c + 32], 16777216.0))
            rt(v, v.tensor_mul(l3(MSK), l3(W1c),
                               rbuf[:, W2c:W2c + 32][:, :, None].broadcast_to([128, 32, 8])))
            rt(v, v.tensor_scalar_mul(rbuf[:, W1c:W1c + 256], rbuf[:, MSK:MSK + 256], -4.0))
            rt(v, v.tensor_scalar_add(rbuf[:, W2c:W2c + 256], rbuf[:, W1c:W1c + 256], 1.0))
            rt(v, v.tensor_relu(rbuf[:, W1c:W1c + 256], rbuf[:, W2c:W2c + 256]))
            rt(v, v.tensor_scalar_mul(rbuf[:, MSK:MSK + 256], rbuf[:, OH1:OH1 + 256], -1.0))
            rt(v, v.tensor_scalar_add(rbuf[:, W2c:W2c + 256], rbuf[:, MSK:MSK + 256], 1.0))
            rt(v, v.tensor_mul(rbuf[:, MSK:MSK + 256], rbuf[:, W1c:W1c + 256], rbuf[:, W2c:W2c + 256]))
            for st_, (src_, dst_) in zip((1, 2, 4), ((MSK, W1c), (W1c, W2c), (W2c, W1c))):
                rt(v, v.tensor_copy(rbuf[:, dst_:dst_ + 256].rearrange("p (t e) -> p t e", e=8)[:, :, 0:st_],
                                    l3(src_)[:, :, 0:st_]))
                rt(v, v.tensor_max(l3(dst_)[:, :, st_:8], l3(src_)[:, :, st_:8],
                                   l3(src_)[:, :, 0:8 - st_]))
            rt(v, v.tensor_scalar_mul(rbuf[:, W2c:W2c + 256], rbuf[:, W1c:W1c + 256], -1.0))
            rt(v, v.tensor_scalar_add(rbuf[:, W1c:W1c + 256], rbuf[:, W2c:W2c + 256], 1.0))
            rt(v, v.tensor_copy(l3(OH2)[:, :, 0:1], l3(MSK)[:, :, 0:1]))
            rt(v, v.tensor_mul(l3(OH2)[:, :, 1:8], l3(MSK)[:, :, 1:8], l3(W1c)[:, :, 0:7]))
        blk.vector(_)
        DV_OH = em.cur(s_dv)

        def _(t):
            t.wait_ge(s_dv, DV_OH)
            for tg in range(NT):
                t.matmul(p_b[0:8, tg:tg + 1], l3(OH1)[:, tg, :], onesf, start=True, stop=True)
                mm = t.matmul(p_b[0:8, 32 + tg:33 + tg], l3(OH2)[:, tg, :], onesf,
                              start=True, stop=True)
            em.fin(mm, s_pe)
        blk.tensor(_)
        PE_TOT = em.cur(s_pe)

        def _(v):
            v.wait_ge(s_pe, PE_TOT)
            em.fin(v.tensor_copy(hrow[0:8, 0:64], p_b[0:8, 0:64]), s_dv)
        blk.vector(_)
        DV_TOT = em.cur(s_dv)

        def _(t):
            t.wait_ge(s_dv, DV_TOT)
            t.transpose(p_c[0:32, 0:8], hrow[0:8, 0:32], identf[0:8, 0:8])
            em.fin(t.transpose(p_c[0:32, 8:16], hrow[0:8, 32:64], identf[0:8, 0:8]), s_pe)
        blk.tensor(_)

        def _(v):
            v.wait_ge(s_pe, em.cur(s_pe))
            em.fin(v.tensor_copy(hrow[0:32, 64:80], p_c[0:32, 0:16]), s_dv)
        blk.vector(_)
        DV_TOTT = em.cur(s_dv)

        def _(t, DV_TOTT=DV_TOTT):
            t.wait_ge(s_dv, DV_TOTT)
            TR = trils[0:32, 0:32]
            TI = identf[0:32, 0:32]
            t1T = hrow[0:32, 64:72]
            t2T = hrow[0:32, 72:80]
            t.matmul(p_d[0:32, 0:8], TR, t1T, start=True, stop=False)
            t.matmul(p_d[0:32, 0:8], onesrowf[0:1, 0:32], crow[0:1, 16:24], start=False, stop=True)
            t.matmul(p_d[0:32, 8:16], TR, t2T, start=True, stop=False)
            t.matmul(p_d[0:32, 8:16], onesrowf[0:1, 0:32], crow[0:1, 32:40], start=False, stop=True)
            t.matmul(p_d[0:32, 16:24], TR, t1T, start=True, stop=False)
            t.matmul(p_d[0:32, 16:24], TR, t2T, start=False, stop=False)
            t.matmul(p_d[0:32, 16:24], onesrowf[0:1, 0:32], crow[0:1, 40:48], start=False, stop=True)
            t.matmul(p_d[0:32, 24:32], TR, t1T, start=True, stop=False)
            t.matmul(p_d[0:32, 24:32], TI, t1T, start=False, stop=False)
            t.matmul(p_d[0:32, 24:32], TR, t2T, start=False, stop=False)
            em.fin(t.matmul(p_d[0:32, 24:32], onesrowf[0:1, 0:32], crow[0:1, 40:48],
                            start=False, stop=True), s_pe)
        blk.tensor(_)

        def _(v):
            v.wait_ge(s_pe, em.cur(s_pe))
            em.fin(v.tensor_copy(hrow[0:32, 80:112], p_d[0:32, 0:32]), s_dv)
        blk.vector(_)
        DV_BASES = em.cur(s_dv)

        def _(g):
            g.wait_ge(s_dv, DV_BASES)
            em.fin(g.dma_start(out=crow[0:1, 0:256].rearrange("p (t e) -> p t e", e=8),
                               in_=hrow[0:32, 80:88][:, None, :]), s_gd, 16)
            em.fin(g.dma_start(out=crow[0:1, 256:512].rearrange("p (t e) -> p t e", e=8),
                               in_=hrow[0:32, 88:96][:, None, :]), s_gd, 16)
            em.fin(g.dma_start(out=crow[0:1, 512:768].rearrange("p (t e) -> p t e", e=8),
                               in_=hrow[0:32, 96:104][:, None, :]), s_gd, 16)
            em.fin(g.dma_start(out=crow[0:1, 768:1024].rearrange("p (t e) -> p t e", e=8),
                               in_=hrow[0:32, 104:112][:, None, :]), s_gd, 16)
        blk.gpsimd(_)
        GD_FLAT = em.cur(s_gd)

        def _(t):
            t.wait_ge(s_gd, GD_FLAT)
            t.matmul(p_e[:, 0:256], onesrowf, crow[0:1, 0:256], start=True, stop=True)
            em.fin(t.matmul(p_e[:, 256:512], onesrowf, crow[0:1, 256:512],
                            start=True, stop=True), s_pe)
        blk.tensor(_)

        def _(v):
            v.wait_ge(s_pe, em.cur(s_pe))
            v.tensor_copy(rbuf[:, B1B:B1B + 256], p_e[:, 0:256])
            em.fin(v.tensor_copy(rbuf[:, B2B:B2B + 256], p_e[:, 256:512]), s_dv)
        blk.vector(_)
        DV_BC1 = em.cur(s_dv)

        def _(t):
            t.wait_ge(s_dv, DV_BC1)
            t.matmul(p_e[:, 0:256], onesrowf, crow[0:1, 512:768], start=True, stop=True)
            em.fin(t.matmul(p_e[:, 256:512], onesrowf, crow[0:1, 768:1024],
                            start=True, stop=True), s_pe)
        blk.tensor(_)

        def _(v):
            v.wait_ge(s_pe, em.cur(s_pe))
            v.tensor_copy(rbuf[:, C1B:C1B + 256], p_e[:, 0:256])
            em.fin(v.tensor_copy(rbuf[:, C2B:C2B + 256], p_e[:, 256:512]), s_dv)
        blk.vector(_)
        DV_BC2 = em.cur(s_dv)

        def _(t):
            t.wait_ge(s_dv, DV_BC2)
            for tg in range(NT):
                t.matmul(p_a[:, tg * 8:tg * 8 + 8], trils, l3(OH1)[:, tg, :],
                         start=True, stop=True)
                mm = t.matmul(p_c[:, tg * 8:tg * 8 + 8], trils, l3(OH2)[:, tg, :],
                              start=True, stop=True)
            em.fin(mm, s_pe)
        blk.tensor(_)
        PE_PFX = em.cur(s_pe)

        def _(v):
            v.wait_ge(s_pe, PE_PFX)
            rt(v, v.tensor_add(rbuf[:, W1c:W1c + 256], p_a[:, 0:256], rbuf[:, B1B:B1B + 256]))
            rt(v, v.tensor_mul(rbuf[:, W2c:W2c + 256], rbuf[:, W1c:W1c + 256],
                               rbuf[:, OH1:OH1 + 256]))
            rt(v, v.reduce_sum(rbuf[:, MXc:MXc + 32], l3(W2c), axis=AX))
            rt(v, v.tensor_add(rbuf[:, W1c:W1c + 256], p_a[:, 0:256], rbuf[:, C1B:C1B + 256]))
            rt(v, v.tensor_mul(rbuf[:, W2c:W2c + 256], rbuf[:, W1c:W1c + 256],
                               rbuf[:, OH1:OH1 + 256]))
            rt(v, v.reduce_sum(rbuf[:, SL1:SL1 + 32], l3(W2c), axis=AX))
            rt(v, v.tensor_scalar_mul(rbuf[:, W1c:W1c + 32], rbuf[:, MXc:MXc + 32], -1.0))
            rt(v, v.tensor_scalar_add(rbuf[:, W2c:W2c + 32], rbuf[:, W1c:W1c + 32],
                                      float(C_GLOBAL)))
            rt(v, v.tensor_relu(rbuf[:, W1c:W1c + 32], rbuf[:, W2c:W2c + 32]))
            rt(v, v.tensor_scalar_min(rbuf[:, W2c:W2c + 32], rbuf[:, W1c:W1c + 32], 1.0))
            rt(v, v.tensor_mul(l3(W1c), l3(OH1),
                               eclbc[:, None, :].broadcast_to([128, 32, 8])))
            rt(v, v.reduce_sum(rbuf[:, MXc:MXc + 32], l3(W1c), axis=AX))
            rt(v, v.tensor_sub(rbuf[:, W1c:W1c + 32], rbuf[:, SL1:SL1 + 32],
                               rbuf[:, MXc:MXc + 32]))
            rt(v, v.tensor_scalar_mul(rbuf[:, MXc:MXc + 32], rbuf[:, W1c:W1c + 32], -1.0))
            rt(v, v.tensor_scalar_add(rbuf[:, W1c:W1c + 32], rbuf[:, MXc:MXc + 32],
                                      float(C_LOC)))
            rt(v, v.tensor_relu(rbuf[:, MXc:MXc + 32], rbuf[:, W1c:W1c + 32]))
            rt(v, v.tensor_scalar_min(rbuf[:, W1c:W1c + 32], rbuf[:, MXc:MXc + 32], 1.0))
            rt(v, v.tensor_mul(rbuf[:, W2c:W2c + 32], rbuf[:, W2c:W2c + 32],
                               rbuf[:, W1c:W1c + 32]))
            rt(v, v.tensor_mul(rbuf[:, PK1:PK1 + 32], rbuf[:, CW1c:CW1c + 32],
                               rbuf[:, W2c:W2c + 32]))
            rt(v, v.tensor_scalar_sub(rbuf[:, W1c:W1c + 32], rbuf[:, SL1:SL1 + 32], trash))
            rt(v, v.tensor_mul(rbuf[:, MXc:MXc + 32], rbuf[:, W1c:W1c + 32],
                               rbuf[:, W2c:W2c + 32]))
            rt(v, v.tensor_scalar_add(rbuf[:, SL1:SL1 + 32], rbuf[:, MXc:MXc + 32], trash))
            rt(v, v.tensor_add(rbuf[:, W1c:W1c + 256], p_c[:, 0:256], rbuf[:, B2B:B2B + 256]))
            rt(v, v.tensor_mul(rbuf[:, W2c:W2c + 256], rbuf[:, W1c:W1c + 256],
                               rbuf[:, OH2:OH2 + 256]))
            rt(v, v.reduce_sum(rbuf[:, MXc:MXc + 32], l3(W2c), axis=AX))
            rt(v, v.tensor_add(rbuf[:, W1c:W1c + 256], p_c[:, 0:256], rbuf[:, C2B:C2B + 256]))
            rt(v, v.tensor_mul(rbuf[:, W2c:W2c + 256], rbuf[:, W1c:W1c + 256],
                               rbuf[:, OH2:OH2 + 256]))
            rt(v, v.reduce_sum(rbuf[:, SL2:SL2 + 32], l3(W2c), axis=AX))
            rt(v, v.tensor_scalar_mul(rbuf[:, W1c:W1c + 32], rbuf[:, MXc:MXc + 32], -1.0))
            rt(v, v.tensor_scalar_add(rbuf[:, W2c:W2c + 32], rbuf[:, W1c:W1c + 32],
                                      float(C_GLOBAL)))
            rt(v, v.tensor_relu(rbuf[:, W1c:W1c + 32], rbuf[:, W2c:W2c + 32]))
            rt(v, v.tensor_scalar_min(rbuf[:, W2c:W2c + 32], rbuf[:, W1c:W1c + 32], 1.0))
            rt(v, v.tensor_mul(l3(W1c), l3(OH2),
                               eclbc[:, None, :].broadcast_to([128, 32, 8])))
            rt(v, v.reduce_sum(rbuf[:, MXc:MXc + 32], l3(W1c), axis=AX))
            rt(v, v.tensor_sub(rbuf[:, W1c:W1c + 32], rbuf[:, SL2:SL2 + 32],
                               rbuf[:, MXc:MXc + 32]))
            rt(v, v.tensor_scalar_mul(rbuf[:, MXc:MXc + 32], rbuf[:, W1c:W1c + 32], -1.0))
            rt(v, v.tensor_scalar_add(rbuf[:, W1c:W1c + 32], rbuf[:, MXc:MXc + 32],
                                      float(C_LOC)))
            rt(v, v.tensor_relu(rbuf[:, MXc:MXc + 32], rbuf[:, W1c:W1c + 32]))
            rt(v, v.tensor_scalar_min(rbuf[:, W1c:W1c + 32], rbuf[:, MXc:MXc + 32], 1.0))
            rt(v, v.tensor_mul(rbuf[:, W2c:W2c + 32], rbuf[:, W2c:W2c + 32],
                               rbuf[:, W1c:W1c + 32]))
            rt(v, v.tensor_mul(rbuf[:, PK2:PK2 + 32], rbuf[:, CW2c:CW2c + 32],
                               rbuf[:, W2c:W2c + 32]))
            rt(v, v.tensor_scalar_sub(rbuf[:, W1c:W1c + 32], rbuf[:, SL2:SL2 + 32], trash))
            rt(v, v.tensor_mul(rbuf[:, MXc:MXc + 32], rbuf[:, W1c:W1c + 32],
                               rbuf[:, W2c:W2c + 32]))
            rt(v, v.tensor_scalar_add(rbuf[:, SL2:SL2 + 32], rbuf[:, MXc:MXc + 32], trash))
            rt(v, v.tensor_copy(slot1, rbuf[:, SL1:SL1 + 32]))
            rt(v, v.tensor_copy(slot2, rbuf[:, SL2:SL2 + 32]))
            rt(v, v.tensor_copy(cw1, rbuf[:, PK1:PK1 + 32]))
            rt(v, v.tensor_copy(cw2, rbuf[:, PK2:PK2 + 32]))
        blk.vector(_)
        DV_SLOTS = em.cur(s_dv)

        def _(g):
            g.wait_ge(s_dv, DV_SLOTS)
            em.fin(g.dma_start(out=dbg_b, in_=rbuf[:, 0:4096]), s_gd, 16)
        blk.gpsimd(_)

        sc_done = []
        for tg in range(NT):
            def dld(sy, tg=tg):
                if tg == 0:
                    sy.wait_ge(s_dv, DV_SLOTS)
                if tg >= 2:
                    sy.wait_ge(s_gd, sc_done[tg - 2][1])
                em.fin(sy.dma_start(out=(kbuf if tg % 2 == 0 else vbuf),
                                    in_=xln1_in[tg * 128:(tg + 1) * 128, :]), s_ld, 16)
            blk.sync(dld)
            LD_T = em.cur(s_ld)

            def dsc(g, tg=tg, LD_T=LD_T):
                g.wait_ge(s_ld, LD_T)
                src_t = kbuf if tg % 2 == 0 else vbuf
                g.indirect_dma_start(
                    out=buf_h, out_offset=IndirectOffsetOnAxis(ap=slot1[:, tg:tg + 1], axis=0),
                    in_=src_t, in_offset=None).then_inc(s_gd, 16)
                em.v[s_gd.num] = em.v.get(s_gd.num, 0) + 16
                last = em.fin(g.indirect_dma_start(
                    out=buf_h, out_offset=IndirectOffsetOnAxis(ap=slot2[:, tg:tg + 1], axis=0),
                    in_=src_t, in_offset=None), s_gd, 16)
                sc_done.append((None, last))
            blk.gpsimd(dsc)
        GD_DISP = em.cur(s_gd)

        def _(v):
            v.wait_ge(s_dv, DV_SLOTS)
            em.fin(v.memset(onesrow5, 1.0), s_dv)
        blk.vector(_)
        DV_ONE5 = em.cur(s_dv)

        w1_done = []
        for e in range(E):
            for sg in range(NSG):
                row0 = e * C_LOC + sg * SG

                def btld(sy, e=e, sg=sg, row0=row0):
                    sy.wait_ge(s_gd, GD_DISP)
                    if (e, sg) != (0, 0):
                        sy.wait_ge(s_pe, em.cur(s_pe))
                    for fc in range(FC):
                        em.fin(sy.dma_start(out=bufT[:, fc, :],
                                            in_=buf_h[row0:row0 + SG, fc * 128:(fc + 1) * 128],
                                            transpose=True), s_ld, 16)
                blk.sync(btld)
                LD_BT = em.cur(s_ld)

                for hc in range(32):
                    def w1ld(g, e=e, hc=hc):
                        if len(w1_done) >= 1:
                            g.wait_ge(s_pe, w1_done[-1])
                        em.fin(g.dma_start(
                            out=wstat[:, 0, :, :],
                            in_=w1_in[e, :, hc * 128:(hc + 1) * 128].rearrange(
                                "(c p) h -> p c h", p=128)), s_gd, 16)
                        if hc == 0:
                            em.fin(g.dma_start(out=b1r, in_=b1_in[e:e + 1, :]), s_gd, 16)
                    blk.gpsimd(w1ld)
                    GD_W1 = em.cur(s_gd)

                    def hmm(t, e=e, sg=sg, hc=hc, GD_W1=GD_W1, LD_BT=LD_BT):
                        t.wait_ge(s_gd, GD_W1)
                        if hc == 0:
                            t.wait_ge(s_ld, LD_BT)
                            if (e, sg) != (0, 0):
                                t.wait_ge(s_dv, em.cur(s_dv))
                        if hc >= 1:
                            t.wait_ge(s_ac, em.cur(s_ac))
                        for fc in range(FC):
                            t.matmul(p_a, wstat[:, 0, fc, :], bufT[:, fc, :],
                                     start=(fc == 0), stop=False)
                        if hc == 0 and (e, sg) == (0, 0):
                            t.wait_ge(s_dv, DV_ONE5)
                        mm = t.matmul(p_a, b1r[0:1, hc * 128:(hc + 1) * 128], onesrow5,
                                      start=False, stop=True)
                        w1_done.append(em.fin(mm, s_pe))
                    blk.tensor(hmm)

                    def gele(sc, hc=hc):
                        sc.wait_ge(s_pe, em.cur(s_pe))
                        em.fin(sc.activation(h_s[:, hc, :], p_a, Act.Gelu), s_ac)
                    blk.scalar(gele)
                AC_H = em.cur(s_ac)

                for hc in range(32):
                    def w2ld(g, e=e, hc=hc):
                        g.wait_ge(s_pe, em.cur(s_pe))
                        em.fin(g.dma_start(
                            out=wmov[:, 0, :],
                            in_=w2_in[e, hc * 128:(hc + 1) * 128, :]), s_gd, 16)
                        if hc == 0:
                            em.fin(g.dma_start(out=b2r, in_=b2_in[e:e + 1, :]), s_gd, 16)
                    blk.gpsimd(w2ld)
                    GD_W2 = em.cur(s_gd)

                    def ymm(t, e=e, sg=sg, hc=hc, GD_W2=GD_W2, AC_H=AC_H):
                        t.wait_ge(s_gd, GD_W2)
                        if hc == 0:
                            t.wait_ge(s_ac, AC_H)
                            if (e, sg) != (0, 0):
                                t.wait_ge(s_dv, em.cur(s_dv))
                        for j in range(4):
                            for half in range(2):
                                mm = t.matmul(PS[j * 2 + half], h_s[:, hc, j * 128:(j + 1) * 128],
                                              wmov[:, 0, half * 512:(half + 1) * 512],
                                              start=(hc == 0), stop=False)
                        if hc == 31:
                            for j in range(4):
                                for half in range(2):
                                    mm = t.matmul(PS[j * 2 + half], onesrow,
                                                  b2r[0:1, half * 512:(half + 1) * 512],
                                                  start=False, stop=True)
                        em.fin(mm, s_pe)
                    blk.tensor(ymm)
                PE_Y = em.cur(s_pe)

                ycopy_done = []
                for j in range(4):
                    def yev(v, j=j, PE_Y=PE_Y):
                        if j == 0:
                            v.wait_ge(s_pe, PE_Y)
                        if j >= 2:
                            v.wait_ge(s_gd, em.cur(s_gd))
                        buf2 = kbuf if j % 2 == 0 else vbuf
                        v.tensor_copy(buf2[:, 0:512], PS[j * 2])
                        ycopy_done.append(em.fin(v.tensor_copy(buf2[:, 512:1024],
                                                               PS[j * 2 + 1]), s_dv))
                    blk.vector(yev)

                    def yst(g, j=j, row0=row0):
                        g.wait_ge(s_dv, ycopy_done[j])
                        buf2 = kbuf if j % 2 == 0 else vbuf
                        em.fin(g.dma_start(out=ybuf_h[row0 + j * 128:row0 + (j + 1) * 128, :],
                                           in_=buf2), s_gd, 16)
                    blk.gpsimd(yst)
        GD_Y = em.cur(s_gd)

        def _(sy):
            sy.wait_ge(s_gd, GD_Y)
            em.fin(sy.dma_start(out=R.rearrange("p (n d) -> p n d", n=NT),
                                in_=x1_in.rearrange("(n p) d -> p n d", p=128)), s_ld, 16)
        blk.sync(_)
        LD_R = em.cur(s_ld)

        gth_done = []
        for tg in range(NT):
            def gth(g, tg=tg):
                if tg == 0:
                    g.wait_ge(s_gd, GD_Y)
                if tg >= 1:
                    g.wait_ge(s_dv, em.cur(s_dv))
                g.indirect_dma_start(
                    out=kbuf, out_offset=None,
                    in_=ybuf_h, in_offset=IndirectOffsetOnAxis(ap=slot1[:, tg:tg + 1], axis=0)
                ).then_inc(s_gd, 16)
                em.v[s_gd.num] = em.v.get(s_gd.num, 0) + 16
                gth_done.append(em.fin(g.indirect_dma_start(
                    out=vbuf, out_offset=None,
                    in_=ybuf_h, in_offset=IndirectOffsetOnAxis(ap=slot2[:, tg:tg + 1], axis=0)
                ), s_gd, 16))
            blk.gpsimd(gth)

            def cmb(v, tg=tg):
                v.wait_ge(s_gd, gth_done[tg])
                if tg == 0:
                    v.wait_ge(s_ld, LD_R)
                base = tg * D
                scrC = arena[:, 0:1024].bitcast(F32)
                for hh in range(2):
                    sl = slice(base + hh * 512, base + (hh + 1) * 512)
                    v.tensor_scalar_mul(scrC, kbuf[:, hh * 512:(hh + 1) * 512], cw1[:, tg:tg + 1])
                    v.tensor_add(R[:, sl], R[:, sl], scrC)
                    v.tensor_scalar_mul(scrC, vbuf[:, hh * 512:(hh + 1) * 512], cw2[:, tg:tg + 1])
                    em.fin(v.tensor_add(R[:, sl], R[:, sl], scrC), s_dv)
            blk.vector(cmb)
        DV_X2 = em.cur(s_dv)

        scrC5 = arena[:, 0:1024].bitcast(F32)

        def _(g):
            g.wait_ge(s_dv, DV_X2)
            for t in range(NT):
                em.fin(g.dma_start(out=x2_h[t * 128:(t + 1) * 128, :],
                                   in_=R[:, t * D:(t + 1) * D]), s_gd, 16)
        blk.gpsimd(_)
        GD_SPILL = em.cur(s_gd)

        def _(v):
            v.wait_ge(s_dv, DV_X2)
            for t in range(NT):
                mm = v.reduce_sum(s1s[:, t:t + 1], R[:, t * D:(t + 1) * D], axis=AX)
            em.fin(mm, s_dv)
        blk.vector(_)

        def _(sc):
            sc.wait_ge(s_dv, DV_X2)
            for t in range(NT):
                mm = sc.activation(kbuf, R[:, t * D:(t + 1) * D], Act.Square,
                                   accum_out=s2s[:, t:t + 1])
            em.fin(mm, s_ac)
        blk.scalar(_)

        def _(v):
            v.wait_ge(s_ac, em.cur(s_ac))
            v.wait_ge(s_dv, em.cur(s_dv))

            def rt(instr):
                val = em.fin(instr, s_dv)
                v.wait_ge(s_dv, val)
            rt(v.tensor_scalar_mul(mcol, s1s, 1.0 / D))
            rt(v.tensor_scalar_mul(rcol, s2s, 1.0 / D))
            rt(v.tensor_tensor(s2s, mcol, mcol, Alu.mult))
            rt(v.tensor_sub(rcol, rcol, s2s))
            em.fin(v.tensor_scalar_add(rcol, rcol, EPS_LN), s_dv)
        blk.vector(_)

        def _(sc):
            sc.wait_ge(s_dv, em.cur(s_dv))
            em.fin(sc.activation(s2s, rcol, Act.Sqrt), s_ac)
        blk.scalar(_)

        def _(v):
            v.wait_ge(s_ac, em.cur(s_ac))
            em.fin(v.reciprocal(rcol, s2s), s_dv)
        blk.vector(_)
        DV_LN2 = em.cur(s_dv)

        norm2_done, store2_done = [], []
        for t in range(NT):
            def nrm(v, t=t):
                if t == 0:
                    v.wait_ge(s_dv, DV_LN2)
                if t >= 2:
                    v.wait_ge(s_gd, store2_done[t - 2])
                buf = kbuf if t % 2 == 0 else vbuf
                for hh in range(2):
                    sl = slice(t * D + hh * 512, t * D + (hh + 1) * 512)
                    v.tensor_scalar_sub(scrC5, R[:, sl], mcol[:, t:t + 1])
                    last = v.tensor_scalar_mul(buf[:, hh * 512:(hh + 1) * 512], scrC5,
                                               rcol[:, t:t + 1])
                norm2_done.append(em.fin(last, s_dv))
            blk.vector(nrm)

            def st(g, t=t):
                g.wait_ge(s_dv, norm2_done[t])
                buf = kbuf if t % 2 == 0 else vbuf
                store2_done.append(em.fin(
                    g.dma_start(out=xln_hbm[t * 128:(t + 1) * 128, :], in_=buf[:, 0:D]),
                    s_gd, 16))
            blk.gpsimd(st)
        GD_XLN2 = em.cur(s_gd)
        DV_NRM2 = em.cur(s_dv)

        def emit_tail(copy_x2=True):
            def _(g):
                for sem in (s_gd, s_dv, s_ld, s_ac, s_pe):
                    g.wait_ge(sem, em.cur(sem))
                if copy_x2:
                    em.fin(g.dma_start(out=out_f, in_=x2_h), s_st, 16)
                else:
                    em.fin(g.dma_start(out=dbg_b[0:1, 0:1], in_=dbg_b[0:1, 0:1]),
                           s_st, 16)
            blk.gpsimd(_)

            def _(sy):
                sy.wait_ge(s_st, em.cur(s_st))
                sy.nop()
            blk.sync(_)

        if cut == 'b5':
            emit_tail()
            return nc

        RB = R.bitcast(BF16)
        wqt = RB[:, 0:8192].rearrange("p (c e) -> p c e", c=FC)
        wkt = RB[:, 8192:16384].rearrange("p (c e) -> p c e", c=FC)
        wvt = RB[:, 16384:24576].rearrange("p (c e) -> p c e", c=FC)
        wot = RB[:, 24576:32768].rearrange("p (c e) -> p c e", c=FC)
        xTb = RB[:, 32768:33792].rearrange("p (c t) -> p c t", c=FC)
        qTs = RB[:, 33792:34816].rearrange("p (c t) -> p c t", c=FC)
        ksb = RB[:, 34816:35840]
        vsb = RB[:, 35840:36864]
        vTs = RB[:, 36864:37888].rearrange("p (c t) -> p c t", c=FC)
        k2sb = RB[:, 37888:38912]
        yTs = RB[:, 38912:39936].rearrange("p (c t) -> p c t", c=FC)
        Mbf = RB[:, 39936:40448]
        Cbf = RB[:, 40448:40960]
        Cvbf = RB[:, 40960:41472]
        Sbf = RB[:, 41472:41984]
        identb = RB[:, 41984:42112]
        ysb = RB[:, 42112:43136]
        wrt = RB[:, 0:8192].rearrange("p (c e) -> p c e", c=FC)
        wk2t = RB[:, 8192:16384].rearrange("p (c e) -> p c e", c=FC)
        wv2t = RB[:, 16384:24576].rearrange("p (c e) -> p c e", c=FC)
        xT3 = RB[:, 24576:25600].rearrange("p (c t) -> p c t", c=FC)
        xsh = RB[:, 25600:26624].rearrange("p (c t) -> p c t", c=FC)
        xrbf = RB[:, 26624:27648].rearrange("p (c t) -> p c t", c=FC)
        xkbf = RB[:, 27648:28672].rearrange("p (c t) -> p c t", c=FC)
        rbf = RB[:, 28672:29696]
        kactb = RB[:, 29696:30720].rearrange("p (c t) -> p c t", c=FC)
        pcol = RB[:, 30720:30728]

        x3t = lf32[:, 0:1024]
        nsum = lf32[:, 1024:1040]
        rstd = lf32[:, 1040:1056]
        sraw = lf32[:, 1056:1072]
        kbar = lf32[:, 1072:1088]
        gamv = lf32[:, 1088:1104]
        lamv = lf32[:, 1104:1120]
        rhov = lf32[:, 1120:1136]
        lnrho_t = lf32[:, 1136:1152]
        lnuv = lf32[:, 1152:1168]
        tsc = lf32[:, 1168:1184]
        ksq = lf32[:, 1184:1696]
        wgbc = lf32[:, 1696:1712]
        bgbc = lf32[:, 1712:1728]
        wlbc = lf32[:, 1728:1744]
        blbc = lf32[:, 1744:1760]
        lnrow = lf32[0:16, 1760:1888]
        lcrow = lf32[0:16, 1888:2016]
        lnurow = lf32[0:16, 2016:2144]
        zrow = lf32[0:16, 2144:2272]
        z16 = lf32[0:16, 2272:2400]
        ectrow = lf32[0:16, 2400:2528]
        lcflat = lf32[0:1, 2528:3040]
        ectflat = lf32[0:1, 3040:3552]
        B16 = lf32[0:16, 3552:4064]
        Ssb = lf32[:, 4064:4576]
        Sscr = lf32[:, 4576:5088]
        selg_sb = lf32[0:16, 5088:7136]
        parw_sb = lf32[0:16, 7136:7264]
        selpair_sb = lf32[0:16, 7264:7776]
        xsc2 = lf32[:, 7776:8288]
        negm4 = lf32[:, 8288:8800]
        murs = lf32[:, 8800:8808]
        muks = lf32[:, 8808:8816]
        wgam4 = lf32[0:1, 8816:8832]
        bgam4 = lf32[0:1, 8832:8848]
        wlam4 = lf32[0:1, 8848:8864]
        blam4 = lf32[0:1, 8864:8880]
        tsc2 = lf32[:, 8880:8896]
        wcol4 = lf32[:, 8896:8900]
        murB = lf32[:, 1024:2048]
        omrB = lf32[:, 2048:3072]
        mukB = lf32[:, 3072:4096]
        omkB = lf32[:, 4096:5120]
        onesbig = lf32[:, 5120:5248]
        m1f = lf32[:, 5248:6272]
        m2f = lf32[:, 6272:7296]

        def _(sy):
            sy.wait_ge(s_gd, GD_SPILL)
            sy.wait_ge(s_dv, DV_NRM2)
            for dst, src_ in ((identb, c_identb), (negm4, c_negmask4),
                              (selg_sb, c_selg), (parw_sb, c_parw),
                              (selpair_sb, c_selpair), (wgam4, wgam), (bgam4, bgam),
                              (wlam4, wlam), (blam4, blam),
                              (murs, mur_in[0:1, :].rearrange("o (c p) -> p (o c)", p=128)),
                              (muks, muk_in[0:1, :].rearrange("o (c p) -> p (o c)", p=128))):
                em.fin(sy.dma_start(out=dst, in_=src_,
                                    allow_slow_non_contiguous=True), s_ld, 16)
            for wt_sb, w_h in ((wqt, wq_bf), (wkt, wk_bf), (wvt, wv_bf), (wot, wo_bf)):
                for dc in range(FC):
                    for eh in range(2):
                        em.fin(sy.dma_start(
                            out=wt_sb[:, dc, eh * 512:(eh + 1) * 512],
                            in_=w_h[eh * 512:(eh + 1) * 512, dc * 128:(dc + 1) * 128],
                            transpose=True), s_ld, 16)
        blk.sync(_)
        LD_L1W = em.cur(s_ld)

        def _(v):
            v.wait_ge(s_gd, GD_SPILL)
            v.wait_ge(s_dv, DV_NRM2)
            v.memset(Ssb, 0.0)
            v.memset(Sbf, 0.0)
            em.fin(v.memset(z16, 0.0), s_dv)
        blk.vector(_)
        DV_S0 = em.cur(s_dv)

        def _(t):
            t.wait_ge(s_ld, LD_L1W)
            t.matmul(p_a[:, 0:16], onesrowf, wgam4, start=True, stop=True)
            t.matmul(p_a[:, 16:32], onesrowf, bgam4, start=True, stop=True)
            t.matmul(p_a[:, 32:48], onesrowf, wlam4, start=True, stop=True)
            em.fin(t.matmul(p_a[:, 48:64], onesrowf, blam4, start=True, stop=True), s_pe)
        blk.tensor(_)

        def _(v):
            v.wait_ge(s_pe, em.cur(s_pe))
            v.tensor_copy(wgbc, p_a[:, 0:16])
            v.tensor_copy(bgbc, p_a[:, 16:32])
            v.tensor_copy(wlbc, p_a[:, 32:48])
            em.fin(v.tensor_copy(blbc, p_a[:, 48:64]), s_dv)
        blk.vector(_)
        DV_GBC = em.cur(s_dv)

        if cut == 'b6':
            emit_tail()
            return nc

        ntiles7 = 1 if cut in ('b7t1', 'dump') else NT
        pe_proj_done = [0]
        st3_done = [0]
        for ti in range(ntiles7):
            if ti % NTS == 0:
                def srst(v, ti=ti):
                    if ti > 0:
                        v.wait_ge(s_pe, em.cur(s_pe))
                    v.memset(Ssb, 0.0)
                    em.fin(v.memset(Sbf, 0.0), s_dv)
                blk.vector(srst)
            DV_SRST = em.cur(s_dv)

            def xld(sy, ti=ti, PEP=pe_proj_done[-1], DVP=em.cur(s_dv)):
                sy.wait_ge(s_ld, LD_L1W)
                sy.wait_ge(s_gd, GD_XLN2)
                if ti > 0:
                    sy.wait_ge(s_pe, PEP)
                    sy.wait_ge(s_gd, st3_done[-1])
                    sy.wait_ge(s_dv, DVP)
                for dc in range(FC):
                    em.fin(sy.dma_start(out=xTb[:, dc, :],
                                        in_=xln_hbm[ti * 128:(ti + 1) * 128,
                                                    dc * 128:(dc + 1) * 128],
                                        transpose=True), s_ld, 16)
                em.fin(sy.dma_start(out=x3t, in_=x2_h[ti * 128:(ti + 1) * 128, :]),
                       s_ld, 16)
            blk.sync(xld)
            LD_XT = em.cur(s_ld)

            def proj(t, LD_XT=LD_XT):
                t.wait_ge(s_ld, LD_XT)
                t.wait_ge(s_dv, em.cur(s_dv))
                t.wait_ge(s_ac, em.cur(s_ac))
                for dc in range(FC):
                    st = (dc == 0)
                    sp = (dc == FC - 1)
                    t.matmul(p_c, xTb[:, dc, :], wkt[:, dc, 0:512], start=st, stop=sp)
                    t.matmul(p_d, xTb[:, dc, :], wkt[:, dc, 512:1024], start=st, stop=sp)
                    t.matmul(p_e, xTb[:, dc, :], wvt[:, dc, 0:512], start=st, stop=sp)
                    t.matmul(p_f, xTb[:, dc, :], wvt[:, dc, 512:1024], start=st, stop=sp)
                for ec in range(FC):
                    dst = p_a if ec < 4 else p_b
                    col = (ec % 4) * 128
                    for dc in range(FC):
                        mm = t.matmul(dst[:, col:col + 128],
                                      wqt[:, dc, ec * 128:(ec + 1) * 128],
                                      xTb[:, dc, :], start=(dc == 0), stop=(dc == FC - 1))
                pe_proj_done.append(em.fin(mm, s_pe))
            blk.tensor(proj)
            PE_PROJ = em.cur(s_pe)
            if ti == 0 and cut == 'p1':
                emit_tail()
                return nc

            def evac1(v, PE_PROJ=PE_PROJ):
                v.wait_ge(s_pe, PE_PROJ)
                v.tensor_copy(qTs[:, 0:4, :].rearrange("p c t -> p (c t)"), p_a)
                v.tensor_copy(qTs[:, 4:8, :].rearrange("p c t -> p (c t)"), p_b)
                v.tensor_copy(ksb[:, 0:512], p_c)
                v.tensor_copy(ksb[:, 512:1024], p_d)
                v.tensor_copy(vsb[:, 0:512], p_e)
                v.tensor_copy(vsb[:, 512:1024], p_f)
                v.reduce_sum(sraw[:, 0:8], p_c.rearrange("p (h k) -> p h k", h=8), axis=AX)
                em.fin(v.reduce_sum(sraw[:, 8:16],
                                    p_d.rearrange("p (h k) -> p h k", h=8), axis=AX), s_dv)
            blk.vector(evac1)
            DV_EV1 = em.cur(s_dv)
            if ti == 0 and cut == 'p2a':
                emit_tail()
                return nc

            def sq1(sc, PE_PROJ=PE_PROJ, DV_EV1=DV_EV1):
                sc.wait_ge(s_pe, PE_PROJ)
                sc.wait_ge(s_dv, DV_EV1)
                em.fin(sc.activation(ksq, p_c, Act.Square), s_ac)
            blk.scalar(sq1)

            def red1(v):
                v.wait_ge(s_ac, em.cur(s_ac))
                em.fin(v.reduce_sum(nsum[:, 0:8],
                                    ksq.rearrange("p (h k) -> p h k", h=8), axis=AX), s_dv)
            blk.vector(red1)

            if ti == 0 and cut == 'p2b1':
                emit_tail()
                return nc

            def sq2(sc):
                sc.wait_ge(s_dv, em.cur(s_dv))
                em.fin(sc.activation(ksq, p_d, Act.Square), s_ac)
            blk.scalar(sq2)

            def red2(v):
                v.wait_ge(s_ac, em.cur(s_ac))
                em.fin(v.reduce_sum(nsum[:, 8:16],
                                    ksq.rearrange("p (h k) -> p h k", h=8), axis=AX), s_dv)
            blk.vector(red2)

            if ti == 0 and cut == 'p2b2':
                emit_tail()
                return nc

            def sqrt1(sc):
                sc.wait_ge(s_dv, em.cur(s_dv))
                em.fin(sc.activation(tsc, nsum, Act.Sqrt), s_ac)
            blk.scalar(sqrt1)
            if ti == 0 and cut == 'p2b':
                emit_tail()
                return nc

            def rchain(v, DV_SRST=DV_SRST):
                v.wait_ge(s_ac, em.cur(s_ac))
                v.wait_ge(s_dv, DV_SRST)

                def rt(instr):
                    val = em.fin(instr, s_dv)
                    v.wait_ge(s_dv, val)
                rt(v.tensor_scalar_max(tsc2, tsc, 1e-12))
                rt(v.reciprocal(rstd, tsc2))
                rt(v.tensor_mul(kbar, sraw, rstd))
                rt(v.tensor_scalar_mul(kbar, kbar, 1.0 / 64.0))
                v.wait_ge(s_dv, DV_GBC)
                rt(v.tensor_mul(tsc, kbar, wgbc))
                em.fin(v.tensor_add(tsc, tsc, bgbc), s_dv)
            blk.vector(rchain)

            def sgm1(sc):
                sc.wait_ge(s_dv, em.cur(s_dv))
                em.fin(sc.activation(gamv, tsc, Act.Sigmoid), s_ac)
            blk.scalar(sgm1)
            if ti == 0 and cut == 'p2c':
                emit_tail()
                return nc

            def rchain2(v):
                v.wait_ge(s_ac, em.cur(s_ac))

                def rt(instr):
                    val = em.fin(instr, s_dv)
                    v.wait_ge(s_dv, val)
                rt(v.tensor_mul(tsc, kbar, wlbc))
                em.fin(v.tensor_add(tsc, tsc, blbc), s_dv)
            blk.vector(rchain2)

            def sgm2(sc):
                sc.wait_ge(s_dv, em.cur(s_dv))
                em.fin(sc.activation(lamv, tsc, Act.Sigmoid), s_ac)
            blk.scalar(sgm2)

            def rchain3(v):
                v.wait_ge(s_ac, em.cur(s_ac))

                def rt(instr):
                    val = em.fin(instr, s_dv)
                    v.wait_ge(s_dv, val)
                rt(v.tensor_mul(tsc, gamv, lamv))
                rt(v.tensor_sub(rhov, gamv, tsc))
                em.fin(v.tensor_scalar_max(tsc, rhov, 1e-38), s_dv)
            blk.vector(rchain3)

            def ln1(sc):
                sc.wait_ge(s_dv, em.cur(s_dv))
                em.fin(sc.activation(lnrho_t, tsc, Act.Ln), s_ac)
            blk.scalar(ln1)

            def rchain4(v):
                v.wait_ge(s_ac, em.cur(s_ac))

                def rt(instr):
                    val = em.fin(instr, s_dv)
                    v.wait_ge(s_dv, val)
                rt(v.tensor_scalar(tsc, rhov, -1.0, 1.0, Alu.mult, Alu.add))
                em.fin(v.tensor_mul(tsc, tsc, rstd), s_dv)
            blk.vector(rchain4)

            def ln2(sc):
                sc.wait_ge(s_dv, em.cur(s_dv))
                em.fin(sc.activation(lnuv, tsc, Act.Ln), s_ac)
            blk.scalar(ln2)
            AC_SCAL = em.cur(s_ac)
            if ti == 0 and cut == 'p2':
                emit_tail()
                return nc

            pgb = p_g.bitcast(BF16)

            def trans1(t, AC_SCAL=AC_SCAL):
                t.wait_ge(s_dv, em.cur(s_dv))
                t.wait_ge(s_ac, AC_SCAL)
                for ec in range(FC):
                    t.transpose(pgb[:, ec * 128:(ec + 1) * 128],
                                vsb[:, ec * 128:(ec + 1) * 128], identb)
                t.transpose(p_f[0:16, 0:128], lnrho_t, identf)
                em.fin(t.transpose(p_f[0:16, 128:256], lnuv, identf), s_pe)
            blk.tensor(trans1)
            PE_TR1 = em.cur(s_pe)

            def evac2(v, PE_TR1=PE_TR1, DV_S0=DV_S0):
                v.wait_ge(s_pe, PE_TR1)
                v.wait_ge(s_dv, DV_S0)

                def rt(instr):
                    val = em.fin(instr, s_dv)
                    v.wait_ge(s_dv, val)
                v.tensor_copy(vTs.rearrange("p c t -> p (c t)"), pgb)
                rt(v.tensor_copy(lnrow, p_f[0:16, 0:128]))
                rt(v.tensor_copy(lnurow, p_f[0:16, 128:256]))
                rt(v.tensor_tensor_scan(lcrow, lnrow, z16, 0.0, Alu.add, Alu.add))
                em.fin(v.tensor_sub(zrow, lnurow, lcrow), s_dv)
            blk.vector(evac2)

            def ect1(sc):
                sc.wait_ge(s_dv, em.cur(s_dv))
                em.fin(sc.activation(ectrow, lcrow, Act.Exp), s_ac)
            blk.scalar(ect1)
            AC_ECT = em.cur(s_ac)
            DV_ROWS = em.cur(s_dv)
            if ti == 0 and cut == 'p3':
                emit_tail()
                return nc

            ydone_pe = None
            for g in range(4):
                def flat(sy, g=g, DV_ROWS=DV_ROWS, AC_ECT=AC_ECT):
                    sy.wait_ge(s_dv, DV_ROWS)
                    sy.wait_ge(s_ac, AC_ECT)
                    if g > 0:
                        sy.wait_ge(s_pe, em.cur(s_pe))
                    em.fin(sy.dma_start(
                        out=lcflat[0:1, :].rearrange("p (r c) -> p r c", c=128),
                        in_=lcrow[4 * g:4 * g + 4, :][:, None, :]), s_ld, 16)
                    em.fin(sy.dma_start(
                        out=ectflat[0:1, :].rearrange("p (r c) -> p r c", c=128),
                        in_=ectrow[4 * g:4 * g + 4, :][:, None, :]), s_ld, 16)
                blk.sync(flat)
                LD_FLAT = em.cur(s_ld)

                def emm(t, g=g, LD_FLAT=LD_FLAT, DV_ROWS=DV_ROWS):
                    t.wait_ge(s_ld, LD_FLAT)
                    t.wait_ge(s_dv, DV_ROWS)
                    if g > 0:
                        t.wait_ge(s_ac, em.cur(s_ac))
                        t.wait_ge(s_dv, em.cur(s_dv))
                    t.matmul(p_a, onesrowf, lcflat, start=True, stop=False)
                    t.matmul(p_a, zrow, selg_sb[:, g * 512:(g + 1) * 512],
                             start=False, stop=False)
                    em.fin(t.matmul(p_a, identf, negm4, start=False, stop=True), s_pe)
                blk.tensor(emm)
                PE_E = em.cur(s_pe)
                if ti == 0 and g == 0 and cut == 'p4a':
                    emit_tail()
                    return nc

                def mexp(sc, PE_E=PE_E):
                    sc.wait_ge(s_pe, PE_E)
                    sc.activation(Mbf, p_a, Act.Exp)
                    em.fin(sc.activation(wcol4, p_a[:, 127::128], Act.Exp), s_ac)
                blk.scalar(mexp)
                AC_M = em.cur(s_ac)
                if ti == 0 and g == 0 and cut == 'p4b':
                    emit_tail()
                    return nc

                def pmm(t, g=g, DVW=em.cur(s_dv)):
                    t.wait_ge(s_dv, DVW)
                    for par in (0, 1):
                        pb = par * 64
                        for hh in range(par, 4, 2):
                            h = 4 * g + hh
                            mm = t.matmul(p_b[:, hh * 128:(hh + 1) * 128],
                                          vTs[pb:pb + 64, h // 2, :],
                                          qTs[pb:pb + 64, h // 2, :],
                                          start=True, stop=True)
                        val = em.fin(mm, s_pe)
                        if par == 0:
                            t.wait_ge(s_pe, val)
                blk.tensor(pmm)
                PE_P = em.cur(s_pe)
                if ti == 0 and g == 0 and cut == 'p4c':
                    emit_tail()
                    return nc

                def ectb(t, AC_M=AC_M, LD_FLAT=LD_FLAT):
                    t.wait_ge(s_ac, AC_M)
                    t.wait_ge(s_ld, LD_FLAT)
                    em.fin(t.matmul(p_a, onesrowf, ectflat, start=True, stop=True), s_pe)
                blk.tensor(ectb)
                PE_ECT = em.cur(s_pe)

                def cmul(v, g=g, AC_M=AC_M, PE_ECT=PE_ECT):
                    v.wait_ge(s_ac, AC_M)
                    v.wait_ge(s_pe, PE_ECT)

                    def rt(instr):
                        val = em.fin(instr, s_dv)
                        v.wait_ge(s_dv, val)
                    rt(v.tensor_mul(Cbf, Mbf, p_b))
                    for hh in range(4):
                        h = 4 * g + hh
                        mm = v.tensor_scalar_mul(k2sb[:, h * 64:(h + 1) * 64],
                                                 ksb[:, h * 64:(h + 1) * 64],
                                                 wcol4[:, hh:hh + 1])
                    rt(mm)
                    for hh in range(4):
                        h = 4 * g + hh
                        pb = (h % 2) * 64
                        mm = v.tensor_mul(Cvbf[pb:pb + 64, hh * 128:(hh + 1) * 128],
                                          qTs[pb:pb + 64, h // 2, :],
                                          p_a[pb:pb + 64, hh * 128:(hh + 1) * 128])
                    em.fin(mm, s_dv)
                blk.vector(cmul)
                DV_C = em.cur(s_dv)
                if ti == 0 and g == 0 and cut == 'p4d':
                    emit_tail()
                    return nc
                if ti == 0 and g == 0 and cut == 'dump':
                    def dmp0(gp, DV_C=DV_C):
                        gp.wait_ge(s_dv, DV_C)
                        em.fin(gp.dma_start(out=dbg_b[:, 3072:3584], in_=Mbf), s_gd, 16)
                        em.fin(gp.dma_start(out=dbg_b[:, 2048:2560], in_=Cbf), s_gd, 16)
                    blk.gpsimd(dmp0)

                def ymm(t, g=g, DV_C=DV_C, ti=ti):
                    t.wait_ge(s_dv, DV_C)
                    for hh in range(4):
                        h = 4 * g + hh
                        pb = (h % 2) * 64
                        j = h // 2
                        yb = p_c if h < 8 else p_d
                        col = (h % 8) * 64
                        t.matmul(yb[:, col:col + 64],
                                 Cbf[:, hh * 128:(hh + 1) * 128],
                                 ksb[:, h * 64:(h + 1) * 64], start=True, stop=True)
                        t.matmul(p_g[pb:pb + 64, j * 64:(j + 1) * 64],
                                 vsb[:, h * 64:(h + 1) * 64],
                                 k2sb[:, h * 64:(h + 1) * 64], start=True, stop=True)
                    for hh in (0, 2):
                        h = 4 * g + hh
                        ib = p_e if h < 8 else p_f
                        col = (h % 8) * 64
                        mm = t.matmul(ib[:, col:col + 64],
                                      Cvbf[0:64, hh * 128:(hh + 1) * 128],
                                      Sbf[0:64, (h // 2) * 64:(h // 2) * 64 + 64],
                                      start=True, stop=True)
                    val = em.fin(mm, s_pe)
                    t.wait_ge(s_pe, val)
                    for hh in (1, 3):
                        h = 4 * g + hh
                        ib = p_e if h < 8 else p_f
                        col = (h % 8) * 64
                        mm = t.matmul(ib[:, col:col + 64],
                                      Cvbf[64:128, hh * 128:(hh + 1) * 128],
                                      Sbf[64:128, (h // 2) * 64:(h // 2) * 64 + 64],
                                      start=True, stop=True)
                    em.fin(mm, s_pe)
                blk.tensor(ymm)
                ydone_pe = em.cur(s_pe)
                if ti == 0 and g == 0 and cut == 'dump':
                    def dmpy(v, ydone_pe=ydone_pe):
                        v.wait_ge(s_pe, ydone_pe)
                        em.fin(v.tensor_copy(xsc2[:, 0:256], p_c[:, 0:256]), s_dv)
                    blk.vector(dmpy)

                    def dmpy2(gp):
                        gp.wait_ge(s_dv, em.cur(s_dv))
                        em.fin(gp.dma_start(out=dbg_b[:, 3776:4032],
                                            in_=xsc2[:, 0:256]), s_gd, 16)
                    blk.gpsimd(dmpy2)
                if ti == 0 and cut == 'p4g%d' % g:
                    emit_tail()
                    return nc

            def supd1(v, ydone_pe=ydone_pe):
                v.wait_ge(s_pe, ydone_pe)
                em.fin(v.tensor_scalar_mul(B16, selpair_sb, ectrow[:, 127:128]), s_dv)
            blk.vector(supd1)

            def supd2(t):
                t.wait_ge(s_dv, em.cur(s_dv))
                em.fin(t.matmul(p_h, parw_sb, B16, start=True, stop=True), s_pe)
            blk.tensor(supd2)

            def supd3(v):
                v.wait_ge(s_pe, em.cur(s_pe))

                def rt(instr):
                    val = em.fin(instr, s_dv)
                    v.wait_ge(s_dv, val)
                rt(v.tensor_mul(Ssb, Ssb, p_h))
                rt(v.tensor_add(Ssb, Ssb, p_g))
                v.tensor_copy(Sbf, Ssb)
                rt(v.tensor_copy(xsc2, p_c))
                rt(v.tensor_add(ysb[:, 0:512], xsc2, p_e))
                rt(v.tensor_copy(xsc2, p_d))
                em.fin(v.tensor_add(ysb[:, 512:1024], xsc2, p_f), s_dv)
            blk.vector(supd3)
            DV_YT = em.cur(s_dv)

            def ytr(t, DV_YT=DV_YT):
                t.wait_ge(s_dv, DV_YT)
                for ec in range(FC):
                    mm = t.transpose(pgb[:, ec * 128:(ec + 1) * 128],
                                     ysb[:, ec * 128:(ec + 1) * 128], identb)
                em.fin(mm, s_pe)
            blk.tensor(ytr)

            def yev(v):
                v.wait_ge(s_pe, em.cur(s_pe))
                em.fin(v.tensor_copy(yTs.rearrange("p c t -> p (c t)"), pgb), s_dv)
            blk.vector(yev)
            DV_YT = em.cur(s_dv)
            if ti == 0 and cut == 'p5':
                emit_tail()
                return nc

            def opmm(t, DV_YT=DV_YT):
                t.wait_ge(s_dv, DV_YT)
                for half in range(2):
                    dst = p_g if half == 0 else p_h
                    for ec in range(FC):
                        mm = t.matmul(dst, yTs[:, ec, :],
                                      wot[:, ec, half * 512:(half + 1) * 512],
                                      start=(ec == 0), stop=(ec == FC - 1))
                em.fin(mm, s_pe)
            blk.tensor(opmm)
            PE_OP = em.cur(s_pe)

            def resid(v, PE_OP=PE_OP, LD_XT=LD_XT, ti=ti):
                v.wait_ge(s_pe, PE_OP)
                v.wait_ge(s_ld, LD_XT)

                def rt(instr):
                    val = em.fin(instr, s_dv)
                    v.wait_ge(s_dv, val)
                v.tensor_add(x3t[:, 0:512], x3t[:, 0:512], p_g)
                rt(v.tensor_add(x3t[:, 512:1024], x3t[:, 512:1024], p_h))
                rt(v.reduce_sum(s1s[:, ti:ti + 1], x3t, axis=AX))
            blk.vector(resid)
            DV_X3 = em.cur(s_dv)

            def st3(g, ti=ti, DV_X3=DV_X3):
                g.wait_ge(s_dv, DV_X3)
                st3_done.append(em.fin(
                    g.dma_start(out=out_f[ti * 128:(ti + 1) * 128, :], in_=x3t),
                    s_gd, 16))
            blk.gpsimd(st3)

            def sq3(sc, DV_X3=DV_X3, ti=ti):
                sc.wait_ge(s_dv, DV_X3)
                em.fin(sc.activation(ksq, x3t[:, 0:512], Act.Square,
                                     accum_out=s2s[:, ti:ti + 1]), s_ac)
            blk.scalar(sq3)

            def sq4(sc, ti=ti):
                em.fin(sc.activation(ksq, x3t[:, 512:1024], Act.Square,
                                     accum_out=tsc2[:, 0:1]), s_ac)
            blk.scalar(sq4)

            def lnc(v, ti=ti):
                v.wait_ge(s_ac, em.cur(s_ac))

                def rt(instr):
                    val = em.fin(instr, s_dv)
                    v.wait_ge(s_dv, val)
                rt(v.tensor_add(s2s[:, ti:ti + 1], s2s[:, ti:ti + 1], tsc2[:, 0:1]))
                rt(v.tensor_scalar_mul(mcol[:, ti:ti + 1], s1s[:, ti:ti + 1], 1.0 / D))
                rt(v.tensor_scalar_mul(rcol[:, ti:ti + 1], s2s[:, ti:ti + 1], 1.0 / D))
                rt(v.tensor_mul(tsc2[:, 1:2], mcol[:, ti:ti + 1], mcol[:, ti:ti + 1]))
                rt(v.tensor_sub(rcol[:, ti:ti + 1], rcol[:, ti:ti + 1], tsc2[:, 1:2]))
                em.fin(v.tensor_scalar_add(rcol[:, ti:ti + 1], rcol[:, ti:ti + 1],
                                           EPS_LN), s_dv)
            blk.vector(lnc)

            def sqr3(sc, ti=ti):
                sc.wait_ge(s_dv, em.cur(s_dv))
                em.fin(sc.activation(tsc2[:, 2:3], rcol[:, ti:ti + 1], Act.Sqrt), s_ac)
            blk.scalar(sqr3)

            def nrm3(v, ti=ti, GDW=em.cur(s_gd)):
                v.wait_ge(s_ac, em.cur(s_ac))
                v.wait_ge(s_gd, GDW)

                def rt(instr):
                    val = em.fin(instr, s_dv)
                    v.wait_ge(s_dv, val)
                rt(v.reciprocal(rcol[:, ti:ti + 1], tsc2[:, 2:3]))
                for hh in range(2):
                    rt(v.tensor_scalar_sub(xsc2, x3t[:, hh * 512:(hh + 1) * 512],
                                           mcol[:, ti:ti + 1]))
                    rt(v.tensor_scalar_mul(kbuf[:, hh * 512:(hh + 1) * 512], xsc2,
                                           rcol[:, ti:ti + 1]))
            blk.vector(nrm3)
            DV_LN3 = em.cur(s_dv)

            def st4(g, ti=ti, DV_LN3=DV_LN3):
                g.wait_ge(s_dv, DV_LN3)
                em.fin(g.dma_start(out=xln3_h[ti * 128:(ti + 1) * 128, :], in_=kbuf),
                       s_gd, 16)
            blk.gpsimd(st4)

            if ti == 0 and cut == 'dump':
                def dmp1(gp, DV_LN3=DV_LN3):
                    gp.wait_ge(s_dv, DV_LN3)
                    em.fin(gp.dma_start(out=dbg_b[:, 0:1024], in_=ksb), s_gd, 16)
                    em.fin(gp.dma_start(out=dbg_b[:, 1024:2048],
                                        in_=qTs.rearrange("p c t -> p (c t)")), s_gd, 16)
                    em.fin(gp.dma_start(out=dbg_b[:, 2560:3072],
                                        in_=yTs[:, 0:4, :].rearrange("p c t -> p (c t)")),
                           s_gd, 16)
                    em.fin(gp.dma_start(out=dbg_b[:, 3584:3600], in_=lnrho_t), s_gd, 16)
                    em.fin(gp.dma_start(out=dbg_b[:, 3600:3616], in_=lnuv), s_gd, 16)
                    em.fin(gp.dma_start(out=dbg_b[:, 3616:3632], in_=rstd), s_gd, 16)
                    em.fin(gp.dma_start(out=dbg_b[:, 3632:3648], in_=kbar), s_gd, 16)
                    em.fin(gp.dma_start(out=dbg_b[0:16, 3648:3776], in_=lcrow), s_gd, 16)
                blk.gpsimd(dmp1)
        GD_B7 = em.cur(s_gd)

        if cut in ('b7', 'b7t1', 'dump'):
            emit_tail(copy_x2=False)
            return nc

        def _(sy):
            sy.wait_ge(s_pe, em.cur(s_pe))
            sy.wait_ge(s_gd, GD_B7)
            for wt_sb, w_h in ((wrt, wr_bf), (wk2t, wk2_bf), (wv2t, wv2_bf)):
                for dc in range(FC):
                    for eh in range(2):
                        em.fin(sy.dma_start(
                            out=wt_sb[:, dc, eh * 512:(eh + 1) * 512],
                            in_=w_h[eh * 512:(eh + 1) * 512, dc * 128:(dc + 1) * 128],
                            transpose=True), s_ld, 16)
        blk.sync(_)
        LD_FW = em.cur(s_ld)

        def _(v):
            v.wait_ge(s_pe, em.cur(s_pe))
            v.wait_ge(s_ld, LD_L1W)

            def rt(instr):
                val = em.fin(instr, s_dv)
                v.wait_ge(s_dv, val)
            rt(v.memset(onesbig, 1.0))
            for c in range(FC):
                v.tensor_scalar_mul(murB[:, c * 128:(c + 1) * 128], onesbig,
                                    murs[:, c:c + 1])
                mm = v.tensor_scalar_mul(mukB[:, c * 128:(c + 1) * 128], onesbig,
                                         muks[:, c:c + 1])
            rt(mm)
            rt(v.tensor_scalar(omrB, murB, -1.0, 1.0, Alu.mult, Alu.add))
            em.fin(v.tensor_scalar(omkB, mukB, -1.0, 1.0, Alu.mult, Alu.add), s_dv)
        blk.vector(_)
        DV_MU = em.cur(s_dv)

        if cut == 'b8':
            emit_tail()
            return nc

        pe_ffn_done = [0]
        stf_done = [0]
        dv_pcol = [DV_MU]
        for ti in range(NT):
            def fld(sy, ti=ti, PEP=pe_ffn_done[-1], DVP=dv_pcol[-1]):
                sy.wait_ge(s_ld, LD_FW)
                sy.wait_ge(s_gd, GD_B7)
                sy.wait_ge(s_dv, DVP)
                if ti > 0:
                    sy.wait_ge(s_pe, PEP)
                    sy.wait_ge(s_gd, stf_done[-1])
                for dc in range(FC):
                    em.fin(sy.dma_start(out=xT3[:, dc, :],
                                        in_=xln3_h[ti * 128:(ti + 1) * 128,
                                                   dc * 128:(dc + 1) * 128],
                                        transpose=True), s_ld, 16)
                em.fin(sy.dma_start(out=x3t, in_=out_f[ti * 128:(ti + 1) * 128, :]),
                       s_ld, 16)
            blk.sync(fld)
            LD_F = em.cur(s_ld)

            def mix(v, ti=ti, LD_F=LD_F, PEW=em.cur(s_pe)):
                v.wait_ge(s_ld, LD_F)
                v.wait_ge(s_dv, DV_MU)
                v.wait_ge(s_pe, PEW)

                def rt(instr):
                    val = em.fin(instr, s_dv)
                    v.wait_ge(s_dv, val)
                if ti % NTS == 0:
                    rt(v.memset(pcol, 0.0))
                v.tensor_copy(xsh[:, :, 1:128], xT3[:, :, 0:127])
                rt(v.tensor_copy(xsh[:, :, 0:1], pcol[:, :, None]))
                rt(v.tensor_copy(pcol[:, :, None], xT3[:, :, 127:128]))
                v.tensor_mul(m1f, xT3.rearrange("p c t -> p (c t)"), murB)
                rt(v.tensor_mul(m2f, xsh.rearrange("p c t -> p (c t)"), omrB))
                rt(v.tensor_add(xrbf.rearrange("p c t -> p (c t)"), m1f, m2f))
                v.tensor_mul(m1f, xT3.rearrange("p c t -> p (c t)"), mukB)
                rt(v.tensor_mul(m2f, xsh.rearrange("p c t -> p (c t)"), omkB))
                em.fin(v.tensor_add(xkbf.rearrange("p c t -> p (c t)"), m1f, m2f), s_dv)
            blk.vector(mix)
            dv_pcol.append(em.cur(s_dv))
            DV_MIX = em.cur(s_dv)

            def rkmm(t, DV_MIX=DV_MIX):
                t.wait_ge(s_dv, DV_MIX)
                t.wait_ge(s_ac, em.cur(s_ac))
                for half in range(2):
                    dst = p_a if half == 0 else p_b
                    for c in range(FC):
                        t.matmul(dst, xrbf[:, c, :], wrt[:, c, half * 512:(half + 1) * 512],
                                 start=(c == 0), stop=(c == FC - 1))
                for dc in range(FC):
                    dst = p_c if dc < 4 else p_d
                    col = (dc % 4) * 128
                    for c in range(FC):
                        mm = t.matmul(dst[:, col:col + 128],
                                      wk2t[:, c, dc * 128:(dc + 1) * 128],
                                      xkbf[:, c, :], start=(c == 0), stop=(c == FC - 1))
                em.fin(mm, s_pe)
            blk.tensor(rkmm)
            PE_RK = em.cur(s_pe)

            def sig(sc, PE_RK=PE_RK):
                sc.wait_ge(s_pe, PE_RK)
                sc.activation(rbf[:, 0:512], p_a, Act.Sigmoid)
                em.fin(sc.activation(rbf[:, 512:1024], p_b, Act.Sigmoid), s_ac)
            blk.scalar(sig)

            def kmax(v, PE_RK=PE_RK):
                v.wait_ge(s_pe, PE_RK)

                def rt(instr):
                    val = em.fin(instr, s_dv)
                    v.wait_ge(s_dv, val)
                rt(v.tensor_scalar_max(m1f[:, 0:512], p_c, 0.0))
                em.fin(v.tensor_scalar_max(m1f[:, 512:1024], p_d, 0.0), s_dv)
            blk.vector(kmax)

            def ksqr(sc):
                sc.wait_ge(s_dv, em.cur(s_dv))
                sc.activation(kactb[:, 0:4, :].rearrange("p c t -> p (c t)"),
                              m1f[:, 0:512], Act.Square)
                em.fin(sc.activation(kactb[:, 4:8, :].rearrange("p c t -> p (c t)"),
                                     m1f[:, 512:1024], Act.Square), s_ac)
            blk.scalar(ksqr)
            AC_KA = em.cur(s_ac)

            def vmm(t, AC_KA=AC_KA):
                t.wait_ge(s_ac, AC_KA)
                t.wait_ge(s_dv, em.cur(s_dv))
                for half in range(2):
                    dst = p_e if half == 0 else p_f
                    for dc in range(FC):
                        mm = t.matmul(dst, kactb[:, dc, :],
                                      wv2t[:, dc, half * 512:(half + 1) * 512],
                                      start=(dc == 0), stop=(dc == FC - 1))
                pe_ffn_done.append(em.fin(mm, s_pe))
            blk.tensor(vmm)
            PE_V = em.cur(s_pe)

            def fin(v, PE_V=PE_V, LD_F=LD_F):
                v.wait_ge(s_pe, PE_V)
                v.wait_ge(s_ac, em.cur(s_ac))
                v.wait_ge(s_ld, LD_F)

                def rt(instr):
                    val = em.fin(instr, s_dv)
                    v.wait_ge(s_dv, val)
                v.tensor_mul(m1f[:, 0:512], rbf[:, 0:512], p_e)
                rt(v.tensor_mul(m1f[:, 512:1024], rbf[:, 512:1024], p_f))
                em.fin(v.tensor_add(m2f, x3t, m1f), s_dv)
            blk.vector(fin)
            DV_FIN = em.cur(s_dv)

            def stf(g, ti=ti, DV_FIN=DV_FIN):
                g.wait_ge(s_dv, DV_FIN)
                stf_done.append(em.fin(
                    g.dma_start(out=out_f[ti * 128:(ti + 1) * 128, :], in_=m2f),
                    s_st, 16))
            blk.gpsimd(stf)

        def _(sy):
            sy.wait_ge(s_st, em.cur(s_st))
            sy.nop()
        blk.sync(_)

    return nc


def kernel(**inputs):
    x = np.ascontiguousarray(np.asarray(inputs['L0_attn_x'], np.float32)).reshape(-1, D)
    wqkv = np.asarray(inputs['L0_attn_w_qkv'], np.float32)
    wout = np.asarray(inputs['L0_attn_w_out'], np.float32)
    bout = np.asarray(inputs['L0_attn_b_out'], np.float32)[None, :]
    wg = np.asarray(inputs['L0_ffn_Wg'], np.float32)
    w1 = np.asarray(inputs['L0_ffn_W1'], np.float32)
    b1 = np.asarray(inputs['L0_ffn_b1'], np.float32)
    w2 = np.asarray(inputs['L0_ffn_W2'], np.float32)
    b2 = np.asarray(inputs['L0_ffn_b2'], np.float32)

    consts_a = host_constants()
    consts_a['c_iota8'] = np.broadcast_to(
        np.arange(8, dtype=np.float32), (128, 8)).copy()
    nc_a = build_launch_a()
    in_maps_a = []
    for c in range(NCORE):
        in_maps_a.append(dict(
            x_in=np.ascontiguousarray(x[c * T:(c + 1) * T]),
            wqkv=wqkv, wout=wout, bout=bout, wg=wg, **consts_a))
    res_a = run_bass_kernel_spmd(nc_a, in_maps_a, core_ids=list(range(NCORE)))

    hist_all = np.stack([np.asarray(res_a.results[c]['hist_out'], np.float32)
                         for c in range(NCORE)]).reshape(NCORE, 2 * E)

    consts_b = host_constants_b()
    nc_b = build_launch_b()
    in_maps_b = []
    for c in range(NCORE):
        hp = np.zeros((NCORE, 2 * E), np.float32)
        hp[:c] = hist_all[:c]
        in_maps_b.append(dict(
            x1_in=np.ascontiguousarray(np.asarray(res_a.results[c]['x1_out'],
                                                  np.float32)),
            xln1_in=np.ascontiguousarray(res_a.results[c]['xln1_out']),
            hist_all=hist_all, hist_prev=hp,
            wg=wg, w1_in=w1, b1_in=b1, w2_in=w2, b2_in=b2,
            wq_in=np.asarray(inputs['L1_attn_Wq'], np.float32),
            wk_in=np.asarray(inputs['L1_attn_Wk'], np.float32),
            wv_in=np.asarray(inputs['L1_attn_Wv'], np.float32),
            wo_in=np.asarray(inputs['L1_attn_Wo'], np.float32),
            wgam=np.asarray(inputs['L1_attn_W_gamma'], np.float32)[None, :],
            bgam=np.asarray(inputs['L1_attn_b_gamma'], np.float32)[None, :],
            wlam=np.asarray(inputs['L1_attn_W_lambda'], np.float32)[None, :],
            blam=np.asarray(inputs['L1_attn_b_lambda'], np.float32)[None, :],
            wr_in=np.asarray(inputs['L1_ffn_W_r'], np.float32),
            wk2_in=np.asarray(inputs['L1_ffn_W_k'], np.float32),
            wv2_in=np.asarray(inputs['L1_ffn_W_v'], np.float32),
            mur_in=np.asarray(inputs['L1_ffn_mu_r'], np.float32)[None, :],
            muk_in=np.asarray(inputs['L1_ffn_mu_k'], np.float32)[None, :],
            **consts_b))
    res_b = run_bass_kernel_spmd(nc_b, in_maps_b, core_ids=list(range(NCORE)))

    out = np.concatenate([np.asarray(res_b.results[c]['out_f'], np.float32)
                          for c in range(NCORE)], axis=0)
    return out.reshape(16, 2048, 1024)
